# version 01 (final)
# baseline (speedup 1.0000x reference)
import contextlib
import numpy as np
import bass_rust
import concourse.bass as bass
import concourse.mybir as mybir
from concourse.bass_utils import run_bass_kernel_spmd
from concourse.alu_op_type import AluOpType as ALU

F32 = mybir.dt.float32
BF16 = mybir.dt.bfloat16
I32 = mybir.dt.int32
AF = mybir.ActivationFunctionType
AX = mybir.AxisListType.X

NCORE = 8
T = 2048
D = 1024
BPC = 2
NTOK = BPC * T


import bass_rust as _br


class Chain:

    NP = 4

    def __init__(self):
        self.ops = []

    def op(self, eng, fn):
        self.ops.append((eng, fn, 'c'))

    def dma(self, eng, out, in_, **kw):
        if not isinstance(out, _br.AP):
            out = out[:, :]
        if not isinstance(in_, _br.AP):
            in_ = in_[:, :]
        self.ops.append((eng, lambda E, out=out, in_=in_, kw=kw: E.dma_start(out=out, in_=in_, **kw), eng))

    def build(self, nc):
        engines = sorted(set(e for e, _, _ in self.ops) | {'sync'})
        NP = self.NP
        pools = {'c': 0, 'sync': NP, 'gpsimd': 2 * NP}
        nsem = 3 * NP
        cnt = {k: 0 for k in pools}
        cum = [0] * nsem
        sched = []
        prev = None
        for i, (e, fn, kind) in enumerate(self.ops):
            s = pools[kind] + (cnt[kind] % NP)
            cnt[kind] += 1
            inc = 16 if kind != 'c' else 1
            cum[s] += inc
            sched.append((prev, s, inc))
            prev = (s, cum[s])
        final = prev
        stack = contextlib.ExitStack()
        sems = [stack.enter_context(nc.semaphore(f"ch{k}")) for k in range(nsem)]
        with nc.Block() as block:
            for eng_name in engines:
                def make(eng_name):
                    def body(E):
                        for i, (e, fn, kind) in enumerate(self.ops):
                            if e != eng_name:
                                continue
                            w, s, inc = sched[i]
                            if w is not None:
                                E.wait_ge(sems[w[0]], w[1])
                            fn(E).then_inc(sems[s], inc)
                        if eng_name == 'sync':
                            E.wait_ge(sems[final[0]], final[1])
                    return body
                getattr(block, eng_name)(make(eng_name))
        return stack


class B:

    def __init__(self):
        self.nc = bass.Bass("TRN2", target_bir_lowering=False)
        self.ch = Chain()
        self.stack = contextlib.ExitStack()
        self.stacks = [self.stack]
        self._i = 0
        self.consts = {}

    def push(self):
        self.stacks.append(contextlib.ExitStack())

    def pop(self):
        self.stacks.pop().close()

    def _nm(self):
        self._i += 1
        return f"t{self._i}"

    def sb(self, p, f, dt=F32):
        return self.stacks[-1].enter_context(self.nc.sbuf_tensor(self._nm(), [p, f], dt))

    def psum(self, p, f, dt=F32):
        eb = 4 if dt == F32 else 2
        per = f * eb
        pad = (-per) % 2048
        fa = f + pad // eb
        t = self.stacks[-1].enter_context(self.nc.psum_tensor(self._nm(), [p, fa], dt))
        return t[:, 0:f] if fa != f else t

    def hin(self, name, shape, dt=F32):
        return self.nc.dram_tensor(name, list(shape), dt, kind="ExternalInput")

    def hout(self, name, shape, dt=F32):
        return self.nc.dram_tensor(name, list(shape), dt, kind="ExternalOutput")

    def hscr(self, shape, dt=F32):
        return self.nc.dram_tensor(self._nm() + "_scr", list(shape), dt, kind="Internal")

    def const(self, name, arr, dt=F32):
        if name not in self.consts:
            self.consts[name] = arr
            self._ch = getattr(self, '_ch', {})
            self._ch[name] = self.nc.dram_tensor(name, list(arr.shape), dt, kind="ExternalInput")
        return self._ch[name]

    def load(self, dst, src):
        self.ch.dma('sync', dst, src)

    def store(self, dst, src):
        self.ch.dma('sync', dst, src)

    def cast(self, dst, src, eng='vector'):
        self.ch.op(eng, lambda E, d=_AP(dst), s=_AP(src): E.tensor_copy(d, s))

    def copy(self, dst, src):
        self.ch.op('scalar', lambda E, d=_AP(dst), s=_AP(src): E.copy(d, s))

    def act(self, dst, src, func, bias=..., scale=..., accum=None):
        kw = {}
        if bias is not ...:
            kw['bias'] = bias
        if scale is not ...:
            kw['scale'] = scale
        if accum is not None:
            kw['accum_out'] = accum
        self.ch.op('scalar', lambda E, d=_AP(dst), s=_AP(src), kw=kw: E.activation(d, s, func, **kw))

    def tt(self, dst, a, b, op):
        self.ch.op('vector', lambda E, d=_AP(dst), a=_AP(a), b=_AP(b), op=op: E.tensor_tensor(d, a, b, op=op))

    def ts(self, dst, a, s1, op0, s2=None, op1=None):
        if op1 is None:
            self.ch.op('vector', lambda E, d=_AP(dst), a=_AP(a), s1=s1, op0=op0: E.tensor_scalar(d, a, s1, None, op0=op0))
        else:
            self.ch.op('vector', lambda E, d=_AP(dst), a=_AP(a), s1=s1, s2=s2, op0=op0, op1=op1: E.tensor_scalar(d, a, s1, s2, op0=op0, op1=op1))

    def stt(self, dst, in0, scalar, in1, op0, op1):
        self.ch.op('vector', lambda E, d=_AP(dst), a=_AP(in0), s=scalar, b=_AP(in1), op0=op0, op1=op1: E.scalar_tensor_tensor(d, a, s, b, op0=op0, op1=op1))

    def red(self, dst, src, op='add', negate=False):
        if op == 'add':
            self.ch.op('vector', lambda E, d=_AP(dst), s=_AP(src), n=negate: E.reduce_sum(d, s, axis=AX, negate=n))
        else:
            self.ch.op('vector', lambda E, d=_AP(dst), s=_AP(src), n=negate: E.reduce_max(d, s, axis=AX, negate=n))

    def memset(self, ap, v):
        self.ch.op('gpsimd', lambda E, a=_AP(ap), v=v: E.memset(a, v))

    def mm(self, out, pairs, accum=False):
        def f(E, out=_AP(out), pairs=[(_AP(l), _AP(r)) for l, r in pairs], accum=accum):
            n = len(pairs)
            for i, (l, r) in enumerate(pairs):
                inst = E.matmul(out, l, r, start=(i == 0 and not accum), stop=(i == n - 1), skip_group_check=True)
            return inst
        self.ch.op('tensor', f)

    def scan(self, dst, a, b, init):
        self.ch.op('vector', lambda E, d=_AP(dst), a=_AP(a), b=_AP(b), i=_AP(init): E.tensor_tensor_scan(d, a, b, i, op0=ALU.mult, op1=ALU.add))

    def tr128(self, dst, src, psT, ident):
        src = _AP(src)
        p, f = src.shape[0], src.shape[1]
        self.ch.op('tensor', lambda E, d=psT, s=src, I=ident: E.transpose(d[0:f, 0:p], s, I[:, :]))
        self.cast(dst, psT[0:f, 0:p])

    def bcast_row(self, dst, row, psB, onesf, W, M=128):
        c0 = 0
        while c0 < W:
            c = min(512, W - c0)
            self.mm(psB[0:M, 0:c], [(onesf[0:1, 0:M], row[0:1, c0:c0 + c])])
            self.cast(dst[0:M, c0:c0 + c], psB[0:M, 0:c])
            c0 += 512

    def lnorm(self, xn_out, xt, scr, scr2, mu, var, D_, eps=1e-5):
        self.red(mu, xt, 'add')
        self.ts(mu, mu, 1.0 / D_, ALU.mult)
        self.ts(scr, xt, mu[:, 0:1], ALU.subtract)
        self.act(scr2, scr, AF.Square, accum=var[:, 0:1])
        self.ts(var[:, 0:1], var[:, 0:1], 1.0 / D_, ALU.mult, eps, ALU.add)
        self.act(var[:, 0:1], var[:, 0:1], AF.Ln)
        self.act(var[:, 0:1], var[:, 0:1], AF.Exp, scale=-0.5)
        self.ts(xn_out, scr, var[:, 0:1], ALU.mult)

    def f_round(self, dst, src, it, f1, f2):
        self.ts(f1, src, 0.5, ALU.add)
        self.cast(it, f1)
        self.cast(f2, it)
        self.tt(f1, f1, f2, ALU.subtract)
        self.ts(f1, f1, 0.0, ALU.is_lt)
        self.tt(dst, f2, f1, ALU.subtract)

    def f_frac(self, dst, src, it, f1):
        self.cast(it, src)
        self.cast(f1, it)
        self.tt(dst, src, f1, ALU.subtract)
        self.ts(f1, dst, 0.0, ALU.is_lt)
        self.tt(dst, dst, f1, ALU.add)

    def finish(self):
        s = self.ch.build(self.nc)
        s.close()
        while self.stacks:
            self.stacks.pop().close()
        return self.nc


def _AP(x):
    return x if isinstance(x, bass_rust.AP) else x[:, :]


def _ident_np():
    import ml_dtypes
    return np.eye(128, dtype=ml_dtypes.bfloat16)


def build_ffnq(mode):
    b = B()
    FF = 4096
    swi = (mode == 'swi')
    if swi:
        xinT = b.hin("xinT", [BPC, D, T], BF16)
        xres = b.hin("xres", [BPC, T, D])
        xf = xres.rearrange("b t d -> (b t) d")
    else:
        xin = b.hin("xin", [BPC, T, D])
        xf = xin.rearrange("b t d -> (b t) d")
    W1h = b.hin("W1", [FF, D])
    W2h = b.hin("W2", [D, FF])
    if not swi:
        b1h = b.hin("b1", [FF])
        b2h = b.hin("b2", [D])
    xout = b.hout("xout", [BPC, T, D])
    identh = b.const("cID", _ident_np(), BF16)
    xo = xout.rearrange("b t d -> (b t) d")

    psT = b.psum(128, 128, BF16)
    psH = b.psum(128, 512)
    psO = b.psum(128, 1024)
    ps1 = b.psum(1, 1)
    psI = b.psum(128, 1)
    ident = b.sb(128, 128, BF16)
    b.load(ident[:, :], identh[:, :])
    onesf = b.sb(1, 512)
    b.memset(onesf[:, :], 1.0)
    ones_bf = b.sb(1, 512, BF16)
    b.memset(ones_bf[:, :], 1.0)

    mu = b.sb(128, 1)
    var = b.sb(128, 2)
    scr = b.sb(128, 1024)
    scr2 = b.sb(128, 1024)
    it32 = b.sb(128, 1024, I32)
    onecol = b.sb(128, 1)
    b.memset(onecol[:, :], 1.0)
    w32 = b.sb(128, 1024)
    stage = b.sb(128, 1024, BF16)

    W1qT = [b.sb(128, FF, BF16) for _ in range(D // 128)]
    W2qT = [b.sb(128, D, BF16) for _ in range(FF // 128)]

    def quantize_to(Wh, rows, cols, nel, dstT):
        ntile = rows // 128
        nch = cols // 1024
        if swi:
            for i in range(ntile):
                for c in range(nch):
                    b.load(w32, Wh[i * 128:(i + 1) * 128, c * 1024:(c + 1) * 1024])
                    b.cast(stage, w32)
                    for j in range(8):
                        b.tr128(dstT[c * 8 + j][:, i * 128:(i + 1) * 128], stage[:, j * 128:(j + 1) * 128], psT, ident)
            return
        cols_abs = b.sb(128, ntile)
        for i in range(ntile):
            for c in range(nch):
                b.load(w32[:, :], Wh[i * 128:(i + 1) * 128, c * 1024:(c + 1) * 1024])
                if c == 0:
                    b.act(scr[:, :], w32[:, :], AF.Abs, accum=cols_abs[:, i:i + 1])
                else:
                    b.act(scr[:, :], w32[:, :], AF.Abs, accum=mu[:, 0:1])
                    b.tt(cols_abs[:, i:i + 1], cols_abs[:, i:i + 1], mu[:, 0:1], ALU.add)
        tot = b.sb(128, 1)
        b.red(tot, cols_abs[:, 0:ntile], 'add')
        b.mm(ps1[:, :], [(tot[:, 0:1], onecol[:, 0:1])])
        g = b.sb(1, 2)
        b.copy(g[0:1, 0:1], ps1[:, :])
        if mode == 'l1':
            b.ts(g[0:1, 0:1], g[0:1, 0:1], 1.0 / nel, ALU.mult, 1e-5, ALU.add)
        else:
            b.ts(g[0:1, 0:1], g[0:1, 0:1], 1.0 / nel, ALU.mult, 1e-8, ALU.max)
        b.ch.op('vector', lambda E, g=g: E.reciprocal(g[0:1, 1:2], g[0:1, 0:1]))
        b.mm(psI[:, :], [(onesf[0:1, 0:128], g[0:1, 1:2])])
        inv = b.sb(128, 1)
        b.copy(inv[:, :], psI[:, :])
        for i in range(ntile):
            for c in range(nch):
                b.load(w32[:, :], Wh[i * 128:(i + 1) * 128, c * 1024:(c + 1) * 1024])
                b.ts(scr[:, :], w32[:, :], inv[:, 0:1], ALU.mult, 0.5, ALU.add)
                b.cast(it32[:, :], scr[:, :])
                b.cast(scr2[:, :], it32[:, :])
                b.tt(scr[:, :], scr[:, :], scr2[:, :], ALU.subtract)
                b.ts(scr[:, :], scr[:, :], 0.0, ALU.is_lt)
                b.tt(scr2[:, :], scr2[:, :], scr[:, :], ALU.subtract)
                b.ch.op('vector', lambda E, d=stage, s=scr2: E.tensor_scalar(d[:, :], s[:, :], -1.0, 1.0, op0=ALU.max, op1=ALU.min))
                for j in range(8):
                    b.tr128(dstT[c * 8 + j][:, i * 128:(i + 1) * 128], stage[:, j * 128:(j + 1) * 128], psT, ident)

    quantize_to(W1h, FF, D, FF * D, W1qT)
    quantize_to(W2h, D, FF, FF * D, W2qT)

    if not swi:
        b1r = b.sb(1, FF, BF16)
        for c in range(4):
            b.load(w32[0:1, :], b1h[None, c * 1024:(c + 1) * 1024])
            b.cast(b1r[0:1, c * 1024:(c + 1) * 1024], w32[0:1, :])
        b2r = b.sb(1, D, BF16)
        b.load(w32[0:1, :], b2h[None, :])
        b.cast(b2r[0:1, :], w32[0:1, :])

    xnT = [b.sb(128, 512, BF16) for _ in range(D // 128)]
    xt = b.sb(128, D)
    xnb = b.sb(128, D, BF16)
    actf = {'l1': AF.Relu, 'l3': AF.Gelu, 'swi': AF.Silu}[mode]
    hT = [b.sb(128, 512, BF16) for _ in range(FF // 128)]
    outsb = scr2
    for tc in range(NTOK // 512):
        if swi:
            bb = tc // 4
            tl = (tc % 4) * 512
            for j in range(D // 128):
                b.load(xnT[j][:, 0:512], xinT[bb, j * 128:(j + 1) * 128, tl:tl + 512])
        else:
            for tt_ in range(4):
                m = tc * 4 + tt_
                b.load(xt[:, :], xf[m * 128:(m + 1) * 128, :])
                b.lnorm(xnb[:, :], xt[:, :], scr[:, 0:D], scr2[:, 0:D], mu, var, D)
                for j in range(D // 128):
                    b.tr128(xnT[j][:, tt_ * 128:(tt_ + 1) * 128], xnb[:, j * 128:(j + 1) * 128], psT, ident)
        for fi in range(FF // 128):
            pairs = [(W1qT[k][:, fi * 128:(fi + 1) * 128], xnT[k][:, 0:512]) for k in range(D // 128)]
            if not swi:
                pairs.append((b1r[0:1, fi * 128:(fi + 1) * 128], ones_bf[0:1, 0:512]))
            b.mm(psH[:, :], pairs)
            b.act(hT[fi][:, :], psH[:, :], actf)
        for tt_ in range(4):
            m = tc * 4 + tt_
            for ncn in range(2):
                sl = slice(ncn * 512, (ncn + 1) * 512)
                pairs = [(hT[k][:, tt_ * 128:(tt_ + 1) * 128], W2qT[k][:, sl]) for k in range(FF // 128)]
                if not swi:
                    pairs.append((ones_bf[0:1, 0:128], b2r[0:1, sl]))
                b.mm(psO[:, sl], pairs)
            b.load(xt[:, :], xf[m * 128:(m + 1) * 128, :])
            b.tt(outsb[:, 0:512], psO[:, 0:512], xt[:, 0:512], ALU.add)
            b.tt(outsb[:, 512:1024], psO[:, 512:1024], xt[:, 512:1024], ALU.add)
            b.store(xo[m * 128:(m + 1) * 128, :], outsb[:, :])
    consts = b.consts
    return b.finish(), consts


def build_causal(mode):
    b = B()
    H = 16
    xin = b.hin("xin", [BPC, T, D])
    xout = b.hout("xout", [BPC, T, D])
    if mode == 'l2':
        qwh = b.hin("qw", [D, D]); kwh = b.hin("kw", [D, D])
        vwh = b.hin("vw", [D, D]); owh = b.hin("ow", [D, D])
        srcw = {'q': (qwh, 0), 'k': (kwh, 0), 'v': (vwh, 0), 'o': (owh, 0)}
    else:
        wah = b.hin("wa", [D, 3 * D]); bah = b.hin("ba", [3 * D])
        wph = b.hin("wp", [D, D]); bph = b.hin("bp", [D])
        srcw = {'q': (wah, 0), 'k': (wah, D), 'v': (wah, 2 * D), 'o': (wph, 0)}
    identh = b.const("cID", _ident_np(), BF16)
    iotah = b.const("cIOTA", np.arange(T, dtype=np.float32)[None, :], F32)
    iotach = b.const("cIOTAC", np.arange(128, dtype=np.float32)[:, None], F32)
    maskdh = b.const("cMASKD", np.where(np.arange(128)[None, :] >= np.arange(128)[:, None], 0.0, -1e30).astype(np.float32), F32)

    SHIFT = 32.0
    slopes = [2.0 ** (-8.0 / H * (i + 1)) for i in range(H)]

    ident = b.sb(128, 128, BF16)
    b.load(ident, identh[:, :])
    onesf = b.sb(1, 512)
    b.memset(onesf, 1.0)
    ones_bf = b.sb(1, 512, BF16)
    b.memset(ones_bf, 1.0)
    nshift = b.sb(128, 1)
    b.memset(nshift, -SHIFT)
    maskd = b.sb(128, 128)
    b.load(maskd, maskdh[:, :])
    iotac = b.sb(128, 1)
    b.load(iotac, iotach[:, :])

    psT = b.psum(128, 128, BF16)
    psB = b.psum(128, 512)
    psS = b.psum(128, 512)
    psO = b.psum(128, 2048)
    mu = b.sb(128, 1); var = b.sb(128, 2)
    scr = b.sb(128, 2048)
    scr2 = b.sb(128, 2048)

    projw = [b.sb(128, D, BF16) for _ in range(D // 128)]
    w32 = b.sb(128, D)

    def load_proj(which):
        Wh, off = srcw[which]
        for i in range(D // 128):
            b.load(w32, Wh[i * 128:(i + 1) * 128, off:off + D])
            b.cast(projw[i], w32)

    if mode == 'l3':
        bab = b.sb(1, 3 * D, BF16)
        for c in range(3):
            b.load(w32[0:1, :], bah[None, c * D:(c + 1) * D])
            b.cast(bab[0:1, c * D:(c + 1) * D], w32[0:1, :])
        bpb = b.sb(1, D, BF16)
        b.load(w32[0:1, :], bph[None, :])
        b.cast(bpb[0:1, :], w32[0:1, :])

    iota_bc = b.sb(128, T)
    b.load(scr[0:1, :], iotah[:, :])
    b.bcast_row(iota_bc, scr[0:1, :], psB, onesf, T)

    xt = b.sb(128, D)
    xnb = b.sb(128, D, BF16)
    xnT = [b.sb(128, T, BF16) for _ in range(D // 128)]
    qT = [b.sb(128, T, BF16) for _ in range(D // 128)]
    kT = [b.sb(128, T, BF16) for _ in range(D // 128)]
    vaug = [b.sb(128, H * 65, BF16) for _ in range(T // 128)]
    aex = b.sb(128, T, BF16)
    outT = xnT
    recb = b.sb(64, T)
    outsb = b.sb(128, D)
    ctc = b.sb(128, 1)

    for bb in range(BPC):
        xb = xin[bb]
        xob = xout[bb]
        for m in range(T // 128):
            b.load(xt, xb[m * 128:(m + 1) * 128, :])
            b.lnorm(xnb, xt, scr[:, 0:D], scr2[:, 0:D], mu, var, D)
            for j in range(D // 128):
                b.tr128(xnT[j][:, m * 128:(m + 1) * 128], xnb[:, j * 128:(j + 1) * 128], psT, ident)
        for dst, which, scale, boff in ((qT, 'q', 0.125, 0), (kT, 'k', 1.0, D)):
            load_proj(which)
            for i in range(D // 128):
                for ncn in range(T // 512):
                    pairs = [(projw[k][:, i * 128:(i + 1) * 128], xnT[k][:, ncn * 512:(ncn + 1) * 512]) for k in range(D // 128)]
                    if mode == 'l3':
                        pairs.append((bab[0:1, boff + i * 128: boff + (i + 1) * 128], ones_bf[0:1, 0:512]))
                    b.mm(psB[:, :], pairs)
                    if scale != 1.0:
                        b.act(dst[i][:, ncn * 512:(ncn + 1) * 512], psB[:, :], AF.Copy, scale=scale)
                    else:
                        b.cast(dst[i][:, ncn * 512:(ncn + 1) * 512], psB[:, :])
        load_proj('v')
        for m in range(T // 128):
            for ncn in range(2):
                pairs = [(xnT[k][:, m * 128:(m + 1) * 128], projw[k][:, ncn * 512:(ncn + 1) * 512]) for k in range(D // 128)]
                if mode == 'l3':
                    pairs.append((ones_bf[0:1, 0:128], bab[0:1, 2 * D + ncn * 512: 2 * D + (ncn + 1) * 512]))
                b.mm(psB[:, :], pairs)
                b.cast(scr[:, ncn * 512:(ncn + 1) * 512], psB[:, :])
            b.memset(vaug[m], 1.0)
            vv = vaug[m][:, :].rearrange("p (h c) -> p h c", c=65)
            sv = scr[:, 0:D].rearrange("p (h c) -> p h c", c=64)
            b.cast(vv[:, :, 0:64], sv)
        ioh = scr
        rec = scr2[0:1, :]
        for h in range(H):
            hb = (h * 64) % 128
            ktile = h // 2
            if mode == 'l2':
                b.ts(ioh, iota_bc, -slopes[h], ALU.mult)
            for kt in range(T // 128):
                qlo = kt * 128
                lhsT = kT[ktile][hb:hb + 64, kt * 128:(kt + 1) * 128]
                if mode == 'l2':
                    b.ts(ctc, iotac, slopes[h], ALU.mult, slopes[h] * qlo, ALU.add)
                c0 = qlo
                while c0 < T:
                    cw = min(512 - (c0 % 512), T - c0)
                    b.mm(psS[:, 0:cw], [(lhsT, qT[ktile][hb:hb + 64, c0:c0 + cw])])
                    if mode == 'l2':
                        b.ts(psS[:, 0:cw], psS[:, 0:cw], ctc[:, 0:1], ALU.add)
                        b.tt(psS[:, 0:cw], psS[:, 0:cw], ioh[:, c0:c0 + cw], ALU.add)
                    if c0 == qlo:
                        b.tt(psS[:, 0:128], psS[:, 0:128], maskd, ALU.add)
                    b.act(aex[:, c0:c0 + cw], psS[:, 0:cw], AF.Exp, bias=nshift[:, 0:1])
                    b.mm(psO[0:65, c0:c0 + cw], [(vaug[kt][:, h * 65:(h + 1) * 65], aex[:, c0:c0 + cw])], accum=(kt > 0))
                    c0 += cw
            b.ch.op('vector', lambda E, r=rec, p=psO: E.reciprocal(r, p[64:65, 0:T]))
            b.bcast_row(recb, rec, psB, onesf, T, M=64)
            oti = (h * 64) // 128
            ob = (h * 64) % 128
            b.tt(outT[oti][ob:ob + 64, :], psO[0:64, 0:T], recb[0:64, 0:T], ALU.mult)
        load_proj('o')
        for m in range(T // 128):
            for ncn in range(2):
                pairs = [(outT[k][:, m * 128:(m + 1) * 128], projw[k][:, ncn * 512:(ncn + 1) * 512]) for k in range(D // 128)]
                if mode == 'l3':
                    pairs.append((ones_bf[0:1, 0:128], bpb[0:1, ncn * 512:(ncn + 1) * 512]))
                b.mm(psB[:, :], pairs)
                b.load(xt, xb[m * 128:(m + 1) * 128, :])
                b.tt(outsb[:, ncn * 512:(ncn + 1) * 512], psB[:, :], xt[:, ncn * 512:(ncn + 1) * 512], ALU.add)
            b.store(xob[m * 128:(m + 1) * 128, :], outsb)
    return b.finish(), b.consts


def build_rotor():
    b = B()
    xin = b.hin("xin", [BPC, T, D])
    xout = b.hout("xout", [BPC, T, D])
    wvh = b.hin("wv", [D, D])
    woh = b.hin("wo", [D, D])
    frh = b.hin("fr", [D])
    identh = b.const("cID", _ident_np(), BF16)
    iotach = b.const("cIOTAC", np.arange(128, dtype=np.float32)[:, None], F32)
    triuh = b.const("cTRIU", np.triu(np.ones((128, 128), np.float32)), F32)

    ident = b.sb(128, 128, BF16)
    b.load(ident, identh[:, :])
    iotac = b.sb(128, 1)
    b.load(iotac, iotach[:, :])
    triu_bf = b.sb(128, 128, BF16)
    w32 = b.sb(128, 1024)
    b.load(w32[:, 0:128], triuh[:, :])
    b.cast(triu_bf, w32[:, 0:128])
    full_bf = b.sb(128, 128, BF16)
    b.memset(full_bf, 1.0)
    npi = b.sb(128, 1)
    b.memset(npi, -float(np.pi))

    psT = b.psum(128, 128, BF16)
    psB = b.psum(128, 512)
    psC = b.psum(128, 512)
    psD = b.psum(128, 512)
    mu = b.sb(128, 1); var = b.sb(128, 2)
    scr = b.sb(128, 1024)
    scr2 = b.sb(128, 2048)

    onesf = b.sb(1, 512)
    b.memset(onesf, 1.0)
    rc = [b.sb(128, D, BF16) for _ in range(16)]
    rs = [b.sb(128, D, BF16) for _ in range(16)]
    b.push()
    fb = b.sb(128, 1024)
    b.load(scr[0:1, 0:D], frh[None, :])
    b.bcast_row(fb, scr[0:1, 0:D], psB, onesf, D)
    b.ts(fb, fb, 1.0 / (2 * np.pi), ALU.mult)
    tcol = b.sb(128, 1)
    it32 = b.sb(128, 1024, I32)
    for m in range(16):
        b.ts(tcol, iotac, 1.0, ALU.mult, float(m * 128), ALU.add)
        b.ts(scr[:, 0:D], fb, tcol[:, 0:1], ALU.mult)
        b.f_frac(scr2[:, 0:D], scr[:, 0:D], it32, scr2[:, D:2 * D])
        b.act(scr2[:, 0:D], scr2[:, 0:D], AF.Sin, scale=float(2 * np.pi), bias=npi[:, 0:1])
        b.ts(rs[m], scr2[:, 0:D], -1.0, ALU.mult)
        b.ts(scr[:, 0:D], scr[:, 0:D], 0.25, ALU.add)
        b.f_frac(scr2[:, 0:D], scr[:, 0:D], it32, scr2[:, D:2 * D])
        b.act(scr2[:, 0:D], scr2[:, 0:D], AF.Sin, scale=float(2 * np.pi), bias=npi[:, 0:1])
        b.ts(rc[m], scr2[:, 0:D], -1.0, ALU.mult)
    b.pop()

    WT = [b.sb(128, D, BF16) for _ in range(8)]

    stg_bf = b.sb(128, D, BF16)

    def loadWT2(Wh):
        for i in range(8):
            b.load(w32, Wh[i * 128:(i + 1) * 128, :])
            b.cast(stg_bf, w32)
            for j in range(8):
                b.tr128(WT[j][:, i * 128:(i + 1) * 128], stg_bf[:, j * 128:(j + 1) * 128], psT, ident)

    xt = b.sb(128, D)
    xnb = b.sb(128, D, BF16)
    xnT = [b.sb(128, T, BF16) for _ in range(8)]
    Pc = [b.sb(128, D, BF16) for _ in range(16)]
    Ps = [b.sb(128, D, BF16) for _ in range(16)]
    retT = [b.sb(128, 128, BF16) for _ in range(8)]
    retb = xnb
    outsb = scr2[:, 1024:2048]

    for bb in range(BPC):
        xb = xin[bb]
        xob = xout[bb]
        loadWT2(wvh)
        for m in range(16):
            b.load(xt, xb[m * 128:(m + 1) * 128, :])
            b.lnorm(xnb, xt, scr[:, 0:D], scr2[:, 0:D], mu, var, D)
            for j in range(8):
                b.tr128(xnT[j][:, m * 128:(m + 1) * 128], xnb[:, j * 128:(j + 1) * 128], psT, ident)
        for m in range(16):
            for ncn in range(2):
                b.mm(psB[:, :], [(xnT[k][:, m * 128:(m + 1) * 128], WT[k][:, ncn * 512:(ncn + 1) * 512]) for k in range(8)])
                b.cast(scr[:, ncn * 512:(ncn + 1) * 512], psB[:, :])
            b.tt(Pc[m], scr[:, 0:D], rc[m], ALU.mult)
            b.tt(Ps[m], scr[:, 0:D], rs[m], ALU.mult)
        loadWT2(woh)
        for m in range(16):
            for ncn in range(2):
                sl = slice(ncn * 512, (ncn + 1) * 512)
                pc_pairs = [(full_bf[:, :], Pc[k][:, sl]) for k in range(m)] + [(triu_bf[:, :], Pc[m][:, sl])]
                ps_pairs = [(full_bf[:, :], Ps[k][:, sl]) for k in range(m)] + [(triu_bf[:, :], Ps[m][:, sl])]
                b.mm(psC[:, :], pc_pairs)
                b.mm(psD[:, :], ps_pairs)
                b.tt(scr[:, sl], psC[:, :], rc[m][:, sl], ALU.mult)
                b.tt(scr2[:, sl], psD[:, :], rs[m][:, sl], ALU.mult)
                b.tt(retb[:, sl], scr[:, sl], scr2[:, sl], ALU.add)
            for j in range(8):
                b.tr128(retT[j], retb[:, j * 128:(j + 1) * 128], psT, ident)
            for ncn in range(2):
                sl = slice(ncn * 512, (ncn + 1) * 512)
                b.mm(psB[:, :], [(retT[k], WT[k][:, sl]) for k in range(8)])
                b.load(xt, xb[m * 128:(m + 1) * 128, :])
                b.tt(outsb[:, sl], psB[:, :], xt[:, sl], ALU.add)
            b.store(xob[m * 128:(m + 1) * 128, :], outsb)
    return b.finish(), b.consts


def _rope_tables(width, nheads, hd=64):
    inv = 1.0 / (10000.0 ** (np.arange(0, hd, 2, dtype=np.float32) / hd))
    fr = np.arange(T, dtype=np.float32)[:, None] * inv[None, :]
    cos = np.cos(fr); sin = np.sin(fr)
    C = np.concatenate([cos, cos], axis=1)
    S = np.concatenate([sin, sin], axis=1)
    import ml_dtypes
    return (np.tile(C, (1, nheads)).astype(ml_dtypes.bfloat16),
            np.tile(S, (1, nheads)).astype(ml_dtypes.bfloat16))


def build_gqa():
    b = B()
    H, HD, NKV = 16, 64, 4
    xin = b.hin("xin", [BPC, T, D])
    xoutT = b.hout("xoutT", [BPC, D, T], BF16)
    wqh = b.hin("wq", [D, D])
    wkh = b.hin("wk", [256, D])
    wvh = b.hin("wv", [256, D])
    identh = b.const("cID", _ident_np(), BF16)
    CQn, SQn = _rope_tables(D, 16)
    CKn, SKn = _rope_tables(256, 4)
    cqh = b.const("cCQ", CQn, BF16)
    sqh = b.const("cSQ", SQn, BF16)
    ckh = b.const("cCK", CKn, BF16)
    skh = b.const("cSK", SKn, BF16)
    maskdh = b.const("cMASKD", np.where(np.arange(128)[None, :] >= np.arange(128)[:, None], 0.0, -1e30).astype(np.float32), F32)
    mask1h = b.const("cMW1", np.where(np.arange(128)[:, None] > np.arange(128)[None, :], 0.0, -1e30).astype(np.float32), F32)

    SHIFT = 32.0
    ident = b.sb(128, 128, BF16)
    b.load(ident, identh[:, :])
    onesf = b.sb(1, 512)
    b.memset(onesf, 1.0)
    nshift = b.sb(128, 1)
    b.memset(nshift, -SHIFT)
    maskd = b.sb(128, 128)
    b.load(maskd, maskdh[:, :])
    mask1 = b.sb(128, 128)
    b.load(mask1, mask1h[:, :])

    psT = b.psum(128, 128, BF16)
    psB = b.psum(128, 512)
    psS = b.psum(128, 512)
    psO = b.psum(128, 2048)
    mu = b.sb(128, 1); var = b.sb(128, 2)
    scr = b.sb(128, 2048)
    scr2 = b.sb(128, 2048)
    outT = [b.sb(128, T, BF16) for _ in range(8)]
    xt = b.sb(128, D)
    outsb = b.sb(128, D)
    recb = b.sb(64, T)

    for bb in range(BPC):
        xb = xin[bb]
        b.push()
        xnb = b.sb(128, D, BF16)
        xnT = [b.sb(128, T, BF16) for _ in range(8)]
        w32 = b.sb(128, D)
        WqT = [b.sb(128, D, BF16) for _ in range(8)]
        WkT = [b.sb(128, 256, BF16) for _ in range(8)]
        WvT = [b.sb(128, 256, BF16) for _ in range(8)]
        stg_bf = b.sb(128, D, BF16)
        qTr = [b.sb(128, T, BF16) for _ in range(8)]
        kdup = [b.sb(128, T, BF16) for _ in range(4)]
        vaug = [b.sb(128, 4 * 65, BF16) for _ in range(16)]
        rt1 = b.sb(128, D, BF16)
        rt2 = b.sb(128, D, BF16)
        qrot = b.sb(128, D, BF16)
        aex = b.sb(128, 256, BF16)

        for i in range(8):
            b.load(w32, wqh[i * 128:(i + 1) * 128, :])
            b.cast(stg_bf, w32)
            for j in range(8):
                b.tr128(WqT[j][:, i * 128:(i + 1) * 128], stg_bf[:, j * 128:(j + 1) * 128], psT, ident)
        for Wh, Wd in ((wkh, WkT), (wvh, WvT)):
            for i in range(2):
                b.load(w32, Wh[i * 128:(i + 1) * 128, :])
                b.cast(stg_bf, w32)
                for j in range(8):
                    b.tr128(Wd[j][:, i * 128:(i + 1) * 128], stg_bf[:, j * 128:(j + 1) * 128], psT, ident)
        for m in range(16):
            b.load(xt, xb[m * 128:(m + 1) * 128, :])
            b.lnorm(xnb, xt, scr[:, 0:D], scr2[:, 0:D], mu, var, D)
            for j in range(8):
                b.tr128(xnT[j][:, m * 128:(m + 1) * 128], xnb[:, j * 128:(j + 1) * 128], psT, ident)
        for m in range(16):
            for ncn in range(2):
                b.mm(psB[:, :], [(xnT[k][:, m * 128:(m + 1) * 128], WqT[k][:, ncn * 512:(ncn + 1) * 512]) for k in range(8)])
                b.act(scr[:, ncn * 512:(ncn + 1) * 512], psB[:, :], AF.Copy, scale=0.125)
            b.load(rt1, cqh[m * 128:(m + 1) * 128, :])
            b.load(rt2, sqh[m * 128:(m + 1) * 128, :])
            b.tt(scr2[:, 0:D], scr[:, 0:D], rt1, ALU.mult)
            b.tt(scr[:, 0:D], scr[:, 0:D], rt2, ALU.mult)
            A3 = scr2[:, 0:D].rearrange("p (h c) -> p h c", c=64)
            B3 = scr[:, 0:D].rearrange("p (h c) -> p h c", c=64)
            Q3 = qrot[:, :].rearrange("p (h c) -> p h c", c=64)
            b.tt(Q3[:, :, 0:32], A3[:, :, 0:32], B3[:, :, 32:64], ALU.subtract)
            b.tt(Q3[:, :, 32:64], B3[:, :, 0:32], A3[:, :, 32:64], ALU.add)
            for j in range(8):
                b.tr128(qTr[j][:, m * 128:(m + 1) * 128], qrot[:, j * 128:(j + 1) * 128], psT, ident)
        for m in range(16):
            b.mm(psB[:, 0:256], [(xnT[k][:, m * 128:(m + 1) * 128], WkT[k][:, :]) for k in range(8)])
            b.cast(scr[:, 0:256], psB[:, 0:256])
            b.load(rt1[:, 0:256], ckh[m * 128:(m + 1) * 128, :])
            b.load(rt2[:, 0:256], skh[m * 128:(m + 1) * 128, :])
            b.tt(scr2[:, 0:256], scr[:, 0:256], rt1[:, 0:256], ALU.mult)
            b.tt(scr[:, 0:256], scr[:, 0:256], rt2[:, 0:256], ALU.mult)
            A3 = scr2[:, 0:256].rearrange("p (h c) -> p h c", c=64)
            B3 = scr[:, 0:256].rearrange("p (h c) -> p h c", c=64)
            Q3 = qrot[:, 0:256].rearrange("p (h c) -> p h c", c=64)
            b.tt(Q3[:, :, 0:32], A3[:, :, 0:32], B3[:, :, 32:64], ALU.subtract)
            b.tt(Q3[:, :, 32:64], B3[:, :, 0:32], A3[:, :, 32:64], ALU.add)
            for g in range(4):
                b.ch.op('tensor', lambda E, d=psT, s=qrot[:, g * 64:(g + 1) * 64], I=ident: E.transpose(d[0:64, 0:128], s, I[:, :]))
                b.cast(kdup[g][0:64, m * 128:(m + 1) * 128], psT[0:64, 0:128])
                b.cast(kdup[g][64:128, m * 128:(m + 1) * 128], psT[0:64, 0:128])
            b.mm(psB[:, 0:256], [(xnT[k][:, m * 128:(m + 1) * 128], WvT[k][:, :]) for k in range(8)])
            b.cast(scr[:, 0:256], psB[:, 0:256])
            b.memset(vaug[m], 1.0)
            vv = vaug[m][:, :].rearrange("p (g c) -> p g c", c=65)
            sv = scr[:, 0:256].rearrange("p (g c) -> p g c", c=64)
            b.cast(vv[:, :, 0:64], sv)
        rec = scr2[0:1, :]
        for h in range(H):
            g = h // 4
            hb = (h * 64) % 128
            hti = (h * 64) // 128
            gb = (g * 64) % 128
            gti = (g * 64) // 128
            for kt in range(16):
                qlo = kt * 128
                W_ = min(256, T - qlo)
                lhsT = kdup[g][hb:hb + 64, kt * 128:(kt + 1) * 128]
                b.mm(psS[:, 0:W_], [(lhsT, qTr[hti][hb:hb + 64, qlo:qlo + W_])])
                b.tt(psS[:, 0:128], psS[:, 0:128], maskd, ALU.add)
                if W_ > 128:
                    b.tt(psS[:, 128:256], psS[:, 128:256], mask1, ALU.add)
                b.act(aex[:, 0:W_], psS[:, 0:W_], AF.Exp, bias=nshift[:, 0:1])
                b.mm(psO[0:65, qlo:qlo + 128], [(vaug[kt][:, g * 65:(g + 1) * 65], aex[:, 0:128])], accum=(kt > 0))
                if W_ > 128:
                    b.mm(psO[0:65, qlo + 128:qlo + 256], [(vaug[kt][:, g * 65:(g + 1) * 65], aex[:, 128:256])], accum=False)
            b.ch.op('vector', lambda E, r=rec, p=psO: E.reciprocal(r, p[64:65, 0:T]))
            b.bcast_row(recb, rec, psB, onesf, T, M=64)
            b.tt(outT[hti][hb:hb + 64, :], psO[0:64, 0:T], recb[0:64, 0:T], ALU.mult)
        b.pop()
        for j in range(8):
            b.store(xoutT[bb, j * 128:(j + 1) * 128, :], outT[j])
    return b.finish(), b.consts


def build_moe(mode):
    b = B()
    FI = 2048 if mode == 'l0' else 4096
    NE = 4
    NF = FI // 128
    xin = b.hin("xin", [BPC, T, D])
    acc = b.hin("acc", [BPC, T, D])
    W1h = b.hin("W1", [NE, D, FI])
    b1h = b.hin("b1", [NE, FI])
    W2h = b.hin("W2", [NE, FI, D])
    b2h = b.hin("b2", [NE, D])
    wgh = b.hin("wg", [D, 8])
    if mode == 'l2':
        brh = b.hin("br", [8])
        gah = b.hin("ga", [D])
        beh = b.hin("be", [D])
    xout = b.hout("xout", [BPC, T, D])
    identh = b.const("cID", _ident_np(), BF16)
    idfh = b.const("cIDF", np.eye(128, dtype=np.float32), F32)
    xf = xin.rearrange("b t d -> (b t) d")
    af = acc.rearrange("b t d -> (b t) d")
    xo = xout.rearrange("b t d -> (b t) d")
    y_scr = b.hscr([NTOK, D])
    xnT_scr = b.hscr([D, NTOK], BF16)
    xn_scr = b.hscr([NTOK, D])

    psT = b.psum(128, 128, BF16)
    psTf = b.psum(128, 128)
    psH = b.psum(128, 512)
    psO = b.psum(128, 1024)
    psR = b.psum(128, 8)
    psC = b.psum(1, 1024)

    ident = b.sb(128, 128, BF16)
    b.load(ident, identh[:, :])
    identf = b.sb(128, 128)
    b.load(identf, idfh[:, :])
    onesf = b.sb(1, 512)
    b.memset(onesf, 1.0)
    ones_bf = b.sb(1, 512, BF16)
    b.memset(ones_bf, 1.0)
    mu = b.sb(128, 1); var = b.sb(128, 2)
    scr = b.sb(128, 1024)
    scr2 = b.sb(128, 1024)
    xt = b.sb(128, D)
    xnb = b.sb(128, D, BF16)
    stg = b.sb(128, 128, BF16)
    wgt = b.sb(128, 64)
    for i in range(8):
        b.load(wgt[:, i * 8:(i + 1) * 8], wgh[i * 128:(i + 1) * 128, :])
    wall = b.sb(128, 256)
    swall = b.sb(128, 32)
    r8 = b.sb(128, 8)
    e8 = b.sb(128, 8)
    m1 = b.sb(128, 1)
    m2 = b.sb(128, 1)
    if mode == 'l2':
        gab = b.sb(128, D)
        beb = b.sb(128, D)
        b.load(scr[0:1, 0:D], gah[None, :])
        b.bcast_row(gab, scr[0:1, 0:D], psH, onesf, D)
        b.load(scr[0:1, 0:D], beh[None, :])
        b.bcast_row(beb, scr[0:1, 0:D], psH, onesf, D)
        brrow = b.sb(1, 32)
        b.load(brrow[0:1, 0:8], brh[None, :])
        brb = b.sb(128, 8)
        b.bcast_row(brb, brrow, psH, onesf, 8)
    xnTf = [b.sb(128, 128) for _ in range(8)]

    def _ln_inplace(src):
        b.red(mu, src, 'add')
        b.ts(mu, mu, 1.0 / D, ALU.mult)
        b.ts(src, src, mu[:, 0:1], ALU.subtract)
        b.act(scr2[:, 0:D], src, AF.Square, accum=var[:, 0:1])
        b.ts(var[:, 0:1], var[:, 0:1], 1.0 / D, ALU.mult, 1e-5, ALU.add)
        b.act(var[:, 0:1], var[:, 0:1], AF.Ln)
        b.act(var[:, 0:1], var[:, 0:1], AF.Exp, scale=-0.5)
        b.ts(src, src, var[:, 0:1], ALU.mult)

    for m in range(NTOK // 128):
        b.load(xt, xf[m * 128:(m + 1) * 128, :])
        b.cast(scr[:, 0:D], xt, eng='vector')
        _ln_inplace(scr[:, 0:D])
        if mode == 'l2':
            b.store(xn_scr[m * 128:(m + 1) * 128, :], scr[:, 0:D])
            _ln_inplace(scr[:, 0:D])
            b.tt(scr[:, 0:D], scr[:, 0:D], gab, ALU.mult)
            b.tt(scr[:, 0:D], scr[:, 0:D], beb, ALU.add)
        b.cast(xnb, scr[:, 0:D])
        for j in range(8):
            b.ch.op('tensor', lambda E, sl=scr[:, j * 128:(j + 1) * 128], I=identf: E.transpose(psTf[:, :], sl, I[:, :]))
            b.copy(xnTf[j][:, :], psTf[:, :])
            b.ch.op('tensor', lambda E, sl=xnb[:, j * 128:(j + 1) * 128], I=ident: E.transpose(psT[:, :], sl, I[:, :]))
            b.cast(stg, psT[:, :])
            b.store(xnT_scr[j * 128:(j + 1) * 128, m * 128:(m + 1) * 128], stg)
        b.mm(psR[:, :], [(xnTf[k][:, :], wgt[:, k * 8:(k + 1) * 8]) for k in range(8)])
        b.copy(r8[:, :], psR[:, :])
        if mode == 'l2':
            b.tt(r8[:, :], r8[:, :], brb, ALU.add)
        b.red(m1, r8, 'max', negate=True)
        b.act(e8[:, :], r8[:, :], AF.Exp, bias=m1[:, 0:1])
        b.red(m1, e8, 'add')
        b.ch.op('vector', lambda E: E.reciprocal(m2[:, 0:1], m1[:, 0:1]))
        b.ts(r8[:, :], e8[:, :], m2[:, 0:1], ALU.mult)
        b.red(m1, r8, 'max')
        b.ts(e8[:, :], r8[:, :], m1[:, 0:1], ALU.is_ge)
        b.ts(e8[:, :], e8[:, :], -1.0, ALU.mult, 1.0, ALU.add)
        b.tt(e8[:, :], r8[:, :], e8[:, :], ALU.mult)
        b.red(m2, e8, 'max')
        b.ts(e8[:, :], r8[:, :], m2[:, 0:1], ALU.is_ge)
        b.tt(e8[:, :], r8[:, :], e8[:, :], ALU.mult)
        b.tt(m1, m1, m2, ALU.add)
        if mode == 'l0':
            b.ch.op('vector', lambda E: E.reciprocal(m2[:, 0:1], m1[:, 0:1]))
            b.ts(e8[:, :], e8[:, :], m2[:, 0:1], ALU.mult)
        b.copy(wall[:, m * 8:(m + 1) * 8], e8[:, :])
        b.copy(swall[:, m:m + 1], m1[:, 0:1])

    b.push()
    W1e = [b.sb(128, FI, BF16) for _ in range(8)]
    W2e = [b.sb(128, D, BF16) for _ in range(NF)]
    hT = [b.sb(128, 512, BF16) for _ in range(NF)]
    xcT = [b.sb(128, 512, BF16) for _ in range(8)]
    b1bf = b.sb(1, FI, BF16)
    rb1 = scr2[:, 0:NF]
    rb1b = scr2[:, 64:64 + (NF + 1) // 2].bitcast(BF16)[:, 0:NF]
    b2bf = b.sb(1, D, BF16) if mode == 'l0' else None
    actf = AF.Gelu if mode == 'l0' else AF.Relu
    for e in range(NE):
        for i in range(8):
            b.ch.dma('gpsimd', W1e[i][:, :], W1h[e, i * 128:(i + 1) * 128, :])
        for i in range(NF):
            b.ch.dma('gpsimd', W2e[i][:, :], W2h[e, i * 128:(i + 1) * 128, :])
        for c in range(FI // 1024):
            b.load(scr[0:1, 0:1024], b1h[None, e, c * 1024:(c + 1) * 1024])
            b.cast(b1bf[0:1, c * 1024:(c + 1) * 1024], scr[0:1, 0:1024])
            if mode == 'l2':
                b.act(scr[0:1, 0:1024], scr[0:1, 0:1024], AF.Relu)
                for j in range(8):
                    fi = c * 8 + j
                    b.ch.op('tensor', lambda E, sl=scr[0:1, j * 128:(j + 1) * 128], I=identf: E.transpose(psTf[0:128, 0:1], sl, I[0:1, 0:1]))
                    b.copy(rb1[:, fi:fi + 1], psTf[0:128, 0:1])
        if mode == 'l0':
            b.ch.dma('gpsimd', b2bf[0:1, :], b2h[None, e, :])
        else:
            b.cast(rb1b, rb1)
        for tch in range(NTOK // 512):
            for j in range(8):
                b.load(xcT[j][:, :], xnT_scr[j * 128:(j + 1) * 128, tch * 512:(tch + 1) * 512])
            for fi in range(NF):
                pairs = [(W1e[k][:, fi * 128:(fi + 1) * 128], xcT[k][:, :]) for k in range(8)]
                pairs.append((b1bf[0:1, fi * 128:(fi + 1) * 128], ones_bf[0:1, 0:512]))
                b.mm(psH[:, :], pairs)
                b.act(hT[fi][:, :], psH[:, :], actf)
                if mode == 'l2':
                    b.ts(hT[fi][:, :], hT[fi][:, :], rb1[:, fi:fi + 1], ALU.subtract)
            for tt_ in range(4):
                mg = tch * 4 + tt_
                for ncn in range(2):
                    sl = slice(ncn * 512, (ncn + 1) * 512)
                    pairs = [(hT[k][:, tt_ * 128:(tt_ + 1) * 128], W2e[k][:, sl]) for k in range(NF)]
                    if mode == 'l0':
                        pairs.append((ones_bf[0:1, 0:128], b2bf[0:1, sl]))
                    b.mm(psO[:, sl], pairs)
                    b.ts(scr[:, sl], psO[:, sl], wall[:, mg * 8 + e:mg * 8 + e + 1], ALU.mult)
                if e == 0:
                    b.store(y_scr[mg * 128:(mg + 1) * 128, :], scr[:, 0:D])
                else:
                    b.ch.dma('gpsimd', y_scr[mg * 128:(mg + 1) * 128, :], scr[:, 0:D], accum_op=ALU.add)
        if mode == 'l2':
            for ncn in range(2):
                sl = slice(ncn * 512, (ncn + 1) * 512)
                b.mm(psC[0:1, sl], [(rb1b[:, k:k + 1], W2e[k][:, sl]) for k in range(NF)], accum=(e > 0))
    b.pop()

    if mode == 'l2':
        crow = b.sb(1, 1024)
        b.copy(crow[0:1, :], psC[0:1, :])
        for e in range(NE):
            b.load(scr[0:1, 0:D], b2h[None, e, :])
            b.tt(crow[0:1, :], crow[0:1, :], scr[0:1, 0:D], ALU.add)
        swrow = b.sb(1, 128)
        for m in range(NTOK // 128):
            b.ch.op('tensor', lambda E, sl=swall[:, m:m + 1], I=identf: E.transpose(psTf[0:1, 0:128], sl, I[:, :]))
            b.copy(swrow[0:1, :], psTf[0:1, 0:128])
            for ncn in range(2):
                sl = slice(ncn * 512, (ncn + 1) * 512)
                b.mm(psO[:, sl], [(swrow[0:1, :], crow[0:1, sl])])
                b.copy(scr[:, sl], psO[:, sl])
            b.ch.dma('gpsimd', y_scr[m * 128:(m + 1) * 128, :], scr[:, 0:D], accum_op=ALU.add)

    for m in range(NTOK // 128):
        b.load(xt, af[m * 128:(m + 1) * 128, :])
        b.load(scr[:, 0:D], y_scr[m * 128:(m + 1) * 128, :])
        b.tt(xt, xt, scr[:, 0:D], ALU.add)
        if mode == 'l2':
            b.load(scr[:, 0:D], xn_scr[m * 128:(m + 1) * 128, :])
            b.stt(xt, scr[:, 0:D], 0.5, xt, ALU.mult, ALU.add)
        b.store(xo[m * 128:(m + 1) * 128, :], xt)
    return b.finish(), b.consts


def build_mamba1():
    b = B()
    DI, DS = 2048, 16
    xin = b.hin("xin", [BPC, T, D])
    xout = b.hout("xout", [BPC, T, D])
    winh = b.hin("win", [2 * DI, D])
    cwh = b.hin("cw", [DI, 1, 4])
    cbh = b.hin("cb", [DI])
    wxh = b.hin("wx", [33, DI])
    wdth = b.hin("wdt", [DI, 1])
    bdth = b.hin("bdt", [DI])
    alogh = b.hin("alog", [DS])
    dph = b.hin("dp", [DI])
    wouth = b.hin("wout", [D, DI])
    lngh = b.hin("lng", [DI])
    lnbh = b.hin("lnb", [DI])
    identh = b.const("cID", _ident_np(), BF16)
    z_scr = b.hscr([BPC, T, DI], BF16)
    bcT_scr = b.hscr([BPC, 32, T], BF16)
    dt_scr = b.hscr([BPC, DI, T])

    psT = b.psum(128, 128, BF16)
    psB = b.psum(128, 512)
    ident = b.sb(128, 128, BF16)
    b.load(ident, identh[:, :])
    onesf = b.sb(1, 512)
    b.memset(onesf, 1.0)
    onecol = b.sb(128, 1)
    b.memset(onecol, 1.0)
    mu = b.sb(128, 1); var = b.sb(128, 2)
    scr = b.sb(128, 2048)
    scr2 = b.sb(128, 2048)
    xt = b.sb(128, D)
    xnb = b.sb(128, D, BF16)
    stg = b.sb(128, 1024, BF16)

    wdc = b.sb(128, 16)
    bdc = b.sb(128, 16)
    for i in range(16):
        b.load(wdc[:, i:i + 1], wdth[i * 128:(i + 1) * 128, :])
        b.load(bdc[:, i:i + 1], bdth[i * 128:(i + 1) * 128][:, None])
    arow = b.sb(1, 16)
    b.load(arow[0:1, :], alogh[None, :])
    b.ts(arow[0:1, :], arow[0:1, :], 5.0, ALU.min)
    b.act(arow[0:1, :], arow[0:1, :], AF.Exp)
    b.ts(arow[0:1, :], arow[0:1, :], -1.0, ALU.mult)
    Abc = b.sb(128, 16)
    b.bcast_row(Abc, arow, psB, onesf, 16)
    ccol = [b.sb(128, 4) for _ in range(16)]
    for i in range(16):
        for k in range(4):
            b.ch.dma('sync', ccol[i][:, k:k + 1], cwh[i * 128:(i + 1) * 128, 0, k:k + 1], allow_slow_non_contiguous=True)
    cbc = b.sb(128, 16)
    for i in range(16):
        b.load(cbc[:, i:i + 1], cbh[i * 128:(i + 1) * 128][:, None])
    WxT = [b.sb(128, 33, BF16) for _ in range(16)]
    b.push()
    wxb = b.sb(33, 2048, BF16)
    b.ch.dma('gpsimd', wxb[:, :], wxh[:, :])
    for i in range(16):
        b.ch.op('tensor', lambda E, sl=wxb[0:33, i * 128:(i + 1) * 128], I=ident: E.transpose(psT[0:128, 0:33], sl, I[0:33, 0:33]))
        b.cast(WxT[i][:, :], psT[0:128, 0:33])
    b.pop()

    u_T = [b.sb(128, T, BF16) for _ in range(16)]
    carr = b.sb(128, 256)

    for bb in range(BPC):
        xb = xin[bb]
        xob = xout[bb]
        b.push()
        W_inT = [b.sb(128, 4096, BF16) for _ in range(8)]
        for i in range(32):
            b.ch.dma('gpsimd', stg[:, 0:D], winh[i * 128:(i + 1) * 128, :])
            for j in range(8):
                b.tr128(W_inT[j][:, i * 128:(i + 1) * 128], stg[:, j * 128:(j + 1) * 128], psT, ident)
        xnT = [b.sb(128, T, BF16) for _ in range(8)]
        for m in range(16):
            b.load(xt, xb[m * 128:(m + 1) * 128, :])
            b.lnorm(xnb, xt, scr[:, 0:D], scr2[:, 0:D], mu, var, D)
            for j in range(8):
                b.tr128(xnT[j][:, m * 128:(m + 1) * 128], xnb[:, j * 128:(j + 1) * 128], psT, ident)
        xcp = b.sb(128, 2051, BF16)
        b.memset(xcp[:, 0:3], 0.0)
        for i in range(16):
            for ncn in range(4):
                b.mm(psB[:, :], [(W_inT[k][:, i * 128:(i + 1) * 128], xnT[k][:, ncn * 512:(ncn + 1) * 512]) for k in range(8)])
                b.cast(xcp[:, 3 + ncn * 512:3 + (ncn + 1) * 512], psB[:, :])
            b.ts(scr[:, 0:T], xcp[:, 0:T], ccol[i][:, 0:1], ALU.mult)
            for k in range(1, 4):
                b.stt(scr[:, 0:T], xcp[:, k:k + T], ccol[i][:, k:k + 1], scr[:, 0:T], ALU.mult, ALU.add)
            b.ts(scr[:, 0:T], scr[:, 0:T], cbc[:, i:i + 1], ALU.add)
            b.act(u_T[i][:, :], scr[:, 0:T], AF.Silu)
        for i in range(16, 32):
            for ncn in range(4):
                b.mm(psB[:, :], [(W_inT[k][:, i * 128:(i + 1) * 128], xnT[k][:, ncn * 512:(ncn + 1) * 512]) for k in range(8)])
                b.act(stg[:, 0:512], psB[:, :], AF.Silu)
                for q in range(4):
                    b.ch.op('tensor', lambda E, sl=stg[:, q * 128:(q + 1) * 128], I=ident: E.transpose(psT[:, :], sl, I[:, :]))
                    b.cast(xnb[:, 0:128], psT[:, :])
                    b.store(z_scr[bb, ncn * 512 + q * 128:ncn * 512 + (q + 1) * 128, (i - 16) * 128:(i - 15) * 128], xnb[:, 0:128])
        for m in range(16):
            b.mm(psB[:, 0:33], [(u_T[k][:, m * 128:(m + 1) * 128], WxT[k][:, :]) for k in range(16)])
            b.copy(scr[:, 0:33], psB[:, 0:33])
            b.cast(xnb[:, 0:33], scr[:, 0:33])
            b.ch.op('tensor', lambda E, sl=xnb[:, 0:1], I=ident: E.transpose(psT[0:1, 0:128], sl, I[:, :]))
            b.cast(scr2[0:1, m * 128:(m + 1) * 128], psT[0:1, 0:128])
            b.ch.op('tensor', lambda E, sl=xnb[:, 1:33], I=ident: E.transpose(psT[0:32, 0:128], sl, I[:, :]))
            b.cast(stg[0:32, 0:128], psT[0:32, 0:128])
            b.store(bcT_scr[bb, :, m * 128:(m + 1) * 128], stg[0:32, 0:128])
        dtraw_bc = b.sb(128, T)
        b.bcast_row(dtraw_bc, scr2[0:1, 0:T], psB, onesf, T)
        for i in range(16):
            b.ts(scr[:, 0:T], dtraw_bc, wdc[:, i:i + 1], ALU.mult)
            b.ts(scr[:, 0:T], scr[:, 0:T], bdc[:, i:i + 1], ALU.add)
            b.act(scr2[:, 0:T], scr[:, 0:T], AF.Abs)
            b.act(scr2[:, 0:T], scr2[:, 0:T], AF.Exp, scale=-1.0)
            b.act(scr2[:, 0:T], scr2[:, 0:T], AF.Ln, bias=onecol[:, 0:1])
            b.act(scr[:, 0:T], scr[:, 0:T], AF.Relu)
            b.tt(scr[:, 0:T], scr[:, 0:T], scr2[:, 0:T], ALU.add)
            b.ch.op('vector', lambda E, d=scr[:, 0:T]: E.tensor_scalar(d, d, 1e-4, 10.0, op0=ALU.max, op1=ALU.min))
            b.store(dt_scr[bb, i * 128:(i + 1) * 128, :], scr[:, 0:T])
        b.pop()
        b.memset(carr, 0.0)
        b.push()
        CH = 512
        Bbc = [b.sb(128, CH, BF16) for _ in range(16)]
        Cbc = [b.sb(128, CH, BF16) for _ in range(16)]
        dtt = b.sb(128, CH)
        dut = b.sb(128, CH)
        dA = b.sb(128, CH)
        dBu = b.sb(128, CH)
        hbuf = b.sb(128, CH)
        yT = b.sb(128, CH)
        ytok = [b.sb(128, 2048, BF16) for _ in range(4)]
        rrow = b.sb(1, CH)
        WoT = [b.sb(128, 1024, BF16) for _ in range(16)]
        y2Tt = [b.sb(128, 128, BF16) for _ in range(16)]
        wstg = b.sb(128, 2048, BF16)
        gb_bc = b.sb(128, 2048, BF16)
        bb_bc = b.sb(128, 2048, BF16)
        db_bc = b.sb(128, 2048, BF16)
        for src, dst in ((lngh, gb_bc), (lnbh, bb_bc), (dph, db_bc)):
            b.load(scr[0:1, 0:DI], src[None, :])
            b.bcast_row(dst, scr[0:1, 0:DI], psB, onesf, DI)
        for i in range(8):
            b.ch.dma('gpsimd', wstg[:, :], wouth[i * 128:(i + 1) * 128, :])
            for j in range(16):
                b.tr128(WoT[j][:, i * 128:(i + 1) * 128], wstg[:, j * 128:(j + 1) * 128], psT, ident)
        for tc in range(4):
            tsl = slice(tc * 512, (tc + 1) * 512)
            for n in range(16):
                b.ch.dma('gpsimd', rrow[0:1, :], bcT_scr[bb, n, tsl][None, :])
                b.bcast_row(Bbc[n], rrow, psB, onesf, CH)
                b.ch.dma('gpsimd', rrow[0:1, :], bcT_scr[bb, 16 + n, tsl][None, :])
                b.bcast_row(Cbc[n], rrow, psB, onesf, CH)
            for i in range(16):
                b.load(dtt, dt_scr[bb, i * 128:(i + 1) * 128, tsl])
                b.tt(dut, dtt, u_T[i][:, tsl], ALU.mult)
                for n in range(16):
                    b.ts(dA, dtt, Abc[:, n:n + 1], ALU.mult, -20.0, ALU.max)
                    b.act(dA, dA, AF.Exp)
                    b.tt(dBu, dut, Bbc[n], ALU.mult)
                    b.ch.op('vector', lambda E, d=dBu: E.tensor_scalar(d[:, :], d[:, :], -10.0, 10.0, op0=ALU.max, op1=ALU.min))
                    b.scan(hbuf, dA, dBu, carr[:, i * 16 + n:i * 16 + n + 1])
                    b.cast(carr[:, i * 16 + n:i * 16 + n + 1], hbuf[:, CH - 1:CH], eng='vector')
                    b.tt(hbuf, hbuf, Cbc[n], ALU.mult)
                    if n == 0:
                        b.cast(yT, hbuf, eng='vector')
                    else:
                        b.tt(yT, yT, hbuf, ALU.add)
                for q in range(4):
                    b.cast(stg[:, 0:128], yT[:, q * 128:(q + 1) * 128])
                    b.ch.op('tensor', lambda E, sl=stg[:, 0:128], I=ident: E.transpose(psT[:, :], sl, I[:, :]))
                    b.cast(ytok[q][:, i * 128:(i + 1) * 128], psT[:, :])
            for q in range(4):
                m = tc * 4 + q
                b.red(mu, ytok[q][:, :], 'add')
                b.ts(mu, mu, 1.0 / DI, ALU.mult)
                b.ts(scr[:, 0:DI], ytok[q][:, :], mu[:, 0:1], ALU.subtract)
                b.act(scr2[:, 0:DI], scr[:, 0:DI], AF.Square, accum=var[:, 0:1])
                b.ts(var[:, 0:1], var[:, 0:1], 1.0 / DI, ALU.mult, 1e-5, ALU.add)
                b.act(var[:, 0:1], var[:, 0:1], AF.Ln)
                b.act(var[:, 0:1], var[:, 0:1], AF.Exp, scale=-0.5)
                b.ts(scr[:, 0:DI], scr[:, 0:DI], var[:, 0:1], ALU.mult)
                b.tt(scr[:, 0:DI], scr[:, 0:DI], gb_bc, ALU.mult)
                b.tt(scr[:, 0:DI], scr[:, 0:DI], bb_bc, ALU.add)
                for i in range(16):
                    b.ch.op('tensor', lambda E, sl=u_T[i][:, m * 128:(m + 1) * 128], I=ident: E.transpose(psT[:, :], sl, I[:, :]))
                    b.cast(xnb[:, 0:128], psT[:, :])
                    b.cast(scr2[:, i * 128:(i + 1) * 128], xnb[:, 0:128])
                b.tt(scr2[:, 0:DI], scr2[:, 0:DI], db_bc, ALU.mult)
                b.tt(scr[:, 0:DI], scr[:, 0:DI], scr2[:, 0:DI], ALU.add)
                b.load(stg[:, 0:1024], z_scr[bb, m * 128:(m + 1) * 128, 0:1024])
                b.tt(scr[:, 0:1024], scr[:, 0:1024], stg[:, 0:1024], ALU.mult)
                b.load(stg[:, 0:1024], z_scr[bb, m * 128:(m + 1) * 128, 1024:2048])
                b.tt(scr[:, 1024:2048], scr[:, 1024:2048], stg[:, 0:1024], ALU.mult)
                y2b = scr2[:, 0:DI // 2].bitcast(BF16)
                b.cast(y2b, scr[:, 0:DI])
                for i in range(16):
                    b.ch.op('tensor', lambda E, sl=y2b[:, i * 128:(i + 1) * 128], I=ident: E.transpose(psT[:, :], sl, I[:, :]))
                    b.cast(y2Tt[i][:, :], psT[:, :])
                for ncn in range(2):
                    sl = slice(ncn * 512, (ncn + 1) * 512)
                    b.mm(psB[:, :], [(y2Tt[k][:, :], WoT[k][:, sl]) for k in range(16)])
                    b.load(xt, xb[m * 128:(m + 1) * 128, :])
                    b.tt(scr[:, sl], psB[:, :], xt[:, sl], ALU.add)
                b.store(xob[m * 128:(m + 1) * 128, :], scr[:, 0:D])
        b.pop()
    return b.finish(), b.consts


def build_linattn():
    b = B()
    H, HD = 16, 64
    xin = b.hin("xin", [BPC, T, D])
    xout = b.hout("xout", [BPC, T, D])
    wqkvh = b.hin("wqkv", [D, 3 * D])
    wouth = b.hin("wo", [D, D])
    identh = b.const("cID", _ident_np(), BF16)
    trilh = b.const("cTRIUI", np.triu(np.ones((128, 128), np.float32)).astype(np.float32), F32)

    ident = b.sb(128, 128, BF16)
    b.load(ident, identh[:, :])
    onesf = b.sb(1, 512)
    b.memset(onesf, 1.0)
    msk = b.sb(128, 128)
    b.load(msk, trilh[:, :])
    psT = b.psum(128, 128, BF16)
    psB = b.psum(128, 512)
    psA = b.psum(128, 128)
    psY = b.psum(128, 65)
    psI = b.psum(128, 65)
    psSt = b.psum(64, 65)
    mu = b.sb(128, 1); var = b.sb(128, 2)
    scr = b.sb(128, 2048)
    scr2 = b.sb(128, 2048)
    xt = b.sb(128, D)
    xnb = b.sb(128, D, BF16)
    outsb = scr2[:, 1024:2048]

    fqT = [b.sb(128, T, BF16) for _ in range(8)]
    fkT = [b.sb(128, T, BF16) for _ in range(8)]
    fkv = [b.sb(128, 1024, BF16) for _ in range(16)]
    onecol_bf = b.sb(128, 1, BF16)
    b.memset(onecol_bf, 1.0)
    att = b.sb(128, 128, BF16)
    hstf = b.sb(64, 65)
    hstb = b.sb(128, 65, BF16)
    oT_scr = b.hscr([D, T], BF16)
    rcol = b.sb(128, 1)

    def phi_ops(dst, src):
        b.ts(scr2[:, 0:src.shape[1]], src, 0.0, ALU.min)
        b.act(scr2[:, 0:src.shape[1]], scr2[:, 0:src.shape[1]], AF.Exp)
        b.ts(scr[:, 0:src.shape[1]], src, 1.0, ALU.add)
        b.tt(dst, scr[:, 0:src.shape[1]], scr2[:, 0:src.shape[1]], ALU.max)

    for bb in range(BPC):
        xb = xin[bb]
        xob = xout[bb]
        b.push()
        w32 = b.sb(128, 1024)
        WK = [b.sb(128, 3 * D, BF16) for _ in range(8)]
        for i in range(8):
            for c in range(3):
                b.load(w32, wqkvh[i * 128:(i + 1) * 128, c * D:(c + 1) * D])
                b.cast(WK[i][:, c * D:(c + 1) * D], w32)
        xnT = [b.sb(128, T, BF16) for _ in range(8)]
        for m in range(16):
            b.load(xt, xb[m * 128:(m + 1) * 128, :])
            b.lnorm(xnb, xt, scr[:, 0:D], scr2[:, 0:D], mu, var, D)
            for j in range(8):
                b.tr128(xnT[j][:, m * 128:(m + 1) * 128], xnb[:, j * 128:(j + 1) * 128], psT, ident)
        for c, dst in ((0, fqT), (1, fkT)):
            for i in range(8):
                for ncn in range(4):
                    b.mm(psB[:, :], [(WK[k][:, c * D + i * 128:c * D + (i + 1) * 128], xnT[k][:, ncn * 512:(ncn + 1) * 512]) for k in range(8)])
                    b.copy(scr[:, 1024:1536], psB[:, :])
                    phi_ops(dst[i][:, ncn * 512:(ncn + 1) * 512], scr[:, 1024:1536])
        for m in range(16):
            for ncn in range(2):
                b.mm(psB[:, :], [(xnT[k][:, m * 128:(m + 1) * 128], WK[k][:, 2 * D + ncn * 512:2 * D + (ncn + 1) * 512]) for k in range(8)])
                b.cast(fkv[m][:, ncn * 512:(ncn + 1) * 512], psB[:, :])
        b.pop()
        for h in range(16):
            hb = (h * 64) % 128
            ht = (h * 64) // 128
            b.memset(hstf, 0.0)
            b.cast(hstb[0:64, :], hstf)
            b.cast(hstb[64:128, :], hstf)
            for m in range(16):
                msl = slice(m * 128, (m + 1) * 128)
                b.mm(psA[:, :], [(fkT[ht][hb:hb + 64, msl], fqT[ht][hb:hb + 64, msl])])
                b.tt(scr[:, 0:128], psA[:, :], msk, ALU.mult)
                b.cast(att, scr[:, 0:128])
                b.mm(psY[:, 0:64], [(att, fkv[m][:, h * 64:(h + 1) * 64])])
                b.mm(psY[:, 64:65], [(att, onecol_bf[:, 0:1])])
                b.mm(psI[:, 0:64], [(fqT[ht][hb:hb + 64, msl], hstb[hb:hb + 64, 0:64])])
                b.mm(psI[:, 64:65], [(fqT[ht][hb:hb + 64, msl], hstb[hb:hb + 64, 64:65])])
                b.ch.op('tensor', lambda E, sl=fkT[ht][hb:hb + 64, msl], I=ident, hb=hb: E.transpose(psT[0:128, 0:64], sl, I[hb:hb + 64, hb:hb + 64]))
                b.cast(att[:, 0:64], psT[0:128, 0:64])
                b.mm(psSt[:, 0:64], [(att[:, 0:64], fkv[m][:, h * 64:(h + 1) * 64])])
                b.mm(psSt[:, 64:65], [(att[:, 0:64], onecol_bf[:, 0:1])])
                b.copy(scr[:, 128:193], psI[:, 0:65])
                b.ts(scr[:, 0:1], psY[:, 64:65], 1e-6, ALU.add)
                b.tt(scr[:, 0:1], scr[:, 0:1], scr[:, 192:193], ALU.add)
                b.ch.op('vector', lambda E: E.reciprocal(rcol[:, 0:1], scr[:, 0:1]))
                b.tt(scr[:, 0:64], psY[:, 0:64], scr[:, 128:192], ALU.add)
                b.ts(scr[:, 0:64], scr[:, 0:64], rcol[:, 0:1], ALU.mult)
                b.cast(att[:, 0:64], scr[:, 0:64])
                b.ch.op('tensor', lambda E, sl=att[:, 0:64], I=ident: E.transpose(psT[0:64, 0:128], sl, I[:, :]))
                b.cast(xnb[0:64, 0:128], psT[0:64, 0:128])
                b.store(oT_scr[h * 64:(h + 1) * 64, msl], xnb[0:64, 0:128])
                b.tt(hstf, hstf, psSt[:, :], ALU.add)
                b.cast(hstb[0:64, :], hstf)
                b.cast(hstb[64:128, :], hstf)
        b.push()
        w32 = b.sb(128, 1024)
        WO = [b.sb(128, D, BF16) for _ in range(8)]
        for i in range(8):
            b.load(w32, wouth[i * 128:(i + 1) * 128, :])
            b.cast(WO[i], w32)
        oTt = [b.sb(128, 128, BF16) for _ in range(8)]
        for m in range(16):
            for k in range(8):
                b.load(oTt[k], oT_scr[k * 128:(k + 1) * 128, m * 128:(m + 1) * 128])
            for ncn in range(2):
                sl = slice(ncn * 512, (ncn + 1) * 512)
                b.mm(psB[:, :], [(oTt[k][:, :], WO[k][:, sl]) for k in range(8)])
                b.load(xt, xb[m * 128:(m + 1) * 128, :])
                b.tt(outsb[:, sl], psB[:, :], xt[:, sl], ALU.add)
            b.store(xob[m * 128:(m + 1) * 128, :], outsb)
        b.pop()
    return b.finish(), b.consts


def build_colattn():
    b = B()
    xin = b.hin("xin", [BPC, T, D])
    xout = b.hout("xout", [BPC, T, D])
    mqh = b.hin("mq", [D, 512])
    mkh = b.hin("mk", [D, 512])
    mvh = b.hin("mv", [D, D])
    identh = b.const("cID", _ident_np(), BF16)
    SHIFT = 32.0

    ident = b.sb(128, 128, BF16)
    b.load(ident, identh[:, :])
    nshift = b.sb(128, 1)
    b.memset(nshift, -SHIFT)
    psT = b.psum(128, 128, BF16)
    psB = b.psum(128, 512)
    psS = b.psum(128, 512)
    psOall = b.psum(128, 2048)
    mu = b.sb(128, 1); var = b.sb(128, 2)
    scr = b.sb(128, 2048)
    scr2 = b.sb(128, 2048)
    xt = b.sb(128, D)
    xnb = b.sb(128, D, BF16)
    y_scr = b.hscr([T, D])

    w32 = b.sb(128, 1024)
    MQ = [b.sb(128, 512, BF16) for _ in range(8)]
    MK = [b.sb(128, 512, BF16) for _ in range(8)]
    MV = [b.sb(128, D, BF16) for _ in range(8)]
    for Wh, Wd, cc in ((mqh, MQ, 512), (mkh, MK, 512), (mvh, MV, 1024)):
        for i in range(8):
            b.load(w32[:, 0:cc], Wh[i * 128:(i + 1) * 128, :])
            b.cast(Wd[i][:, 0:cc], w32[:, 0:cc])

    xnT = [b.sb(128, T, BF16) for _ in range(8)]
    xqT = [b.sb(64, T, BF16) for _ in range(8)]
    xkT = [b.sb(64, T, BF16) for _ in range(8)]
    xmv = [b.sb(128, D, BF16) for _ in range(16)]
    aT = b.sb(128, T, BF16)
    rcol = b.sb(128, 1)

    for bb in range(BPC):
        xb = xin[bb]
        xob = xout[bb]
        for m in range(16):
            b.load(xt, xb[m * 128:(m + 1) * 128, :])
            b.lnorm(xnb, xt, scr[:, 0:D], scr2[:, 0:D], mu, var, D)
            for j in range(8):
                b.tr128(xnT[j][:, m * 128:(m + 1) * 128], xnb[:, j * 128:(j + 1) * 128], psT, ident)
        for Wd, dst, scale in ((MQ, xqT, 0.125), (MK, xkT, 1.0)):
            for si in range(8):
                for ncn in range(4):
                    b.mm(psB[0:64, 0:512], [(Wd[k][:, si * 64:(si + 1) * 64], xnT[k][:, ncn * 512:(ncn + 1) * 512]) for k in range(8)])
                    if scale != 1.0:
                        b.act(dst[si][:, ncn * 512:(ncn + 1) * 512], psB[0:64, 0:512], AF.Copy, scale=scale)
                    else:
                        b.cast(dst[si][:, ncn * 512:(ncn + 1) * 512], psB[0:64, 0:512])
        for m in range(16):
            for ncn in range(2):
                b.mm(psB[:, :], [(xnT[k][:, m * 128:(m + 1) * 128], MV[k][:, ncn * 512:(ncn + 1) * 512]) for k in range(8)])
                b.cast(xmv[m][:, ncn * 512:(ncn + 1) * 512], psB[:, :])
        for h in range(8):
            tb = h * 2
            for sc in range(8):
                for tch in range(2):
                    idx = sc * 2 + tch
                    for sq in range(8):
                        b.mm(psS[:, 0:256], [(xkT[sc][:, (tb + tch) * 128:(tb + tch + 1) * 128], xqT[sq][:, tb * 128:(tb + 2) * 128])])
                        b.act(scr[:, sq * 256:(sq + 1) * 256], psS[:, 0:256], AF.Exp, bias=nshift[:, 0:1])
                    b.red(mu, scr[:, 0:2048], 'add')
                    b.ch.op('vector', lambda E: E.reciprocal(rcol[:, 0:1], mu[:, 0:1]))
                    b.ts(aT[:, :], scr[:, 0:2048], rcol[:, 0:1], ALU.mult)
                    for m2 in range(16):
                        b.mm(psOall[:, m2 * 128:(m2 + 1) * 128], [(aT[:, m2 * 128:(m2 + 1) * 128], xmv[tb + tch][:, sc * 128:(sc + 1) * 128])], accum=(idx > 0))
            for m2 in range(16):
                b.copy(scr[:, 0:128], psOall[:, m2 * 128:(m2 + 1) * 128])
                sq = m2 // 2
                tq0 = (m2 % 2) * 128
                dst = y_scr.rearrange("(tq s) d -> tq s d", s=8)[tq0:tq0 + 128, sq, h * 128:(h + 1) * 128]
                b.store(dst, scr[:, 0:128])
        for m in range(16):
            b.load(xt, xb[m * 128:(m + 1) * 128, :])
            b.load(scr[:, 0:D], y_scr[m * 128:(m + 1) * 128, :])
            b.tt(xt, xt, scr[:, 0:D], ALU.add)
            b.store(xob[m * 128:(m + 1) * 128, :], xt)
    return b.finish(), b.consts


def build_ssd():
    b = B()
    DS, NH, DI, P = 64, 16, 2048, 128
    xin = b.hin("xin", [BPC, T, D])
    xout = b.hout("xout", [BPC, T, D])
    nsh = b.hin("ns", [D])
    nbh = b.hin("nb", [D])
    iph = b.hin("ip", [D, 4256])
    alogh = b.hin("alog", [NH])
    dtbh = b.hin("dtb", [NH])
    bsch = b.hin("bsc", [DS])
    csch = b.hin("csc", [DS])
    bbih = b.hin("bbi", [DS])
    cbih = b.hin("cbi", [DS])
    thh = b.hin("thw", [D, 32])
    ddh = b.hin("dd", [NH])
    oph = b.hin("op", [DI, D])
    identh = b.const("cID", _ident_np(), BF16)
    idfh = b.const("cIDF", np.eye(128, dtype=np.float32), F32)
    tfulh = b.const("cTFUL", np.ones((128, 128), np.float32), F32)
    ttrih = b.const("cTTRI", np.triu(np.ones((128, 128), np.float32)), F32)
    mstrih = b.const("cMSTRI", np.triu(np.ones((128, 128), np.float32), 1), F32)
    oneh_h = b.const("cOH127", (np.arange(128) == 127).astype(np.float32)[:, None], F32)

    z_scr = b.hscr([BPC, DI, T], BF16)
    cd_scr = b.hscr([BPC, T + 256, 48])
    abk_scr = b.hscr([BPC, T + 256, 48])
    cdT_scr = b.hscr([BPC, 16, T])
    yT_scr = b.hscr([BPC, DI, T], BF16)

    psT = b.psum(128, 128, BF16)
    psTf = b.psum(128, 128)
    psB = b.psum(128, 512)
    psM = b.psum(128, 128)
    psY = b.psum(128, 128)
    psY2 = b.psum(128, 128)
    psSt = b.psum(64, 128)
    psCs = b.psum(128, 512)

    ident = b.sb(128, 128, BF16)
    b.load(ident, identh[:, :])
    identf = b.sb(128, 128)
    b.load(identf, idfh[:, :])
    onesf = b.sb(1, 512)
    b.memset(onesf, 1.0)
    tful = b.sb(128, 128)
    b.load(tful, tfulh[:, :])
    ttri = b.sb(128, 128)
    b.load(ttri, ttrih[:, :])
    tfulb = b.sb(128, 128, BF16)
    b.cast(tfulb, tful)
    ttrib = b.sb(128, 128, BF16)
    b.cast(ttrib, ttri)
    mstri = b.sb(128, 128)
    b.load(mstri, mstrih[:, :])
    oh127 = b.sb(128, 1)
    b.load(oh127, oneh_h[:, :])
    mu = b.sb(128, 1); var = b.sb(128, 2)
    scr = b.sb(128, 2048)
    scr2 = b.sb(128, 2048)
    xt = b.sb(128, D)
    xnb = b.sb(128, D, BF16)
    it32 = b.sb(128, 64, I32)

    arow = b.sb(1, 16)
    b.load(arow[0:1, :], alogh[None, :])
    b.act(arow[0:1, :], arow[0:1, :], AF.Exp)
    b.ts(arow[0:1, :], arow[0:1, :], -1.0, ALU.mult)
    Abc = b.sb(128, 16)
    b.bcast_row(Abc, arow, psB, onesf, 16)
    dtbb = b.sb(128, 16)
    b.load(scr[0:1, 0:16], dtbh[None, :])
    b.bcast_row(dtbb, scr[0:1, 0:16], psB, onesf, 16)
    Dbc = b.sb(128, 16)
    b.load(scr[0:1, 0:16], ddh[None, :])
    b.bcast_row(Dbc, scr[0:1, 0:16], psB, onesf, 16)
    bscb = b.sb(128, 64)
    cscb = b.sb(128, 64)
    bbib = b.sb(128, 64)
    cbib = b.sb(128, 64)
    for src, dst in ((bsch, bscb), (csch, cscb), (bbih, bbib), (cbih, cbib)):
        b.load(scr[0:1, 0:64], src[None, :])
        b.bcast_row(dst, scr[0:1, 0:64], psB, onesf, 64)
    nsb = b.sb(128, D)
    nbb = b.sb(128, D)
    b.load(scr[0:1, 0:D], nsh[None, :])
    b.bcast_row(nsb, scr[0:1, 0:D], psB, onesf, D)
    b.load(scr[0:1, 0:D], nbh[None, :])
    b.bcast_row(nbb, scr[0:1, 0:D], psB, onesf, D)

    xhb = [b.sb(128, DI, BF16) for _ in range(16)]
    Bb = [b.sb(128, 64, BF16) for _ in range(16)]
    Cb = [b.sb(128, 64, BF16) for _ in range(16)]
    BT = [b.sb(64, 128, BF16) for _ in range(16)]
    CT = [b.sb(64, 128, BF16) for _ in range(16)]
    alc = [b.sb(128, 16) for _ in range(16)]
    kshc = [b.sb(128, 16) for _ in range(16)]
    bashc = [b.sb(128, 16) for _ in range(16)]
    hstf = b.sb(64, 128)
    hstb = b.sb(64, 128, BF16)
    crow = b.sb(1, 128)
    prow = b.sb(1, 128)
    erow = b.sb(1, 128)
    ebc = b.sb(64, 128)
    sc1 = b.sb(1, 2)
    Gb = b.sb(128, 128)
    stb = b.sb(128, 128, BF16)
    cdsh = b.sb(128, 16)
    ztile = b.sb(128, 128, BF16)
    yTt = [b.sb(128, 128, BF16) for _ in range(16)]

    for bb in range(BPC):
        xb = xin[bb]
        xob = xout[bb]
        b.push()
        ipb = [b.sb(128, 2048, BF16) for _ in range(8)]

        def load_ip(c0, cw):
            for i in range(8):
                b.ch.dma('gpsimd', ipb[i][:, 0:cw], iph[i * 128:(i + 1) * 128, c0:c0 + cw])
        thb = [b.sb(128, 32, BF16) for _ in range(8)]
        for i in range(8):
            b.ch.dma('gpsimd', thb[i][:, :], thh[i * 128:(i + 1) * 128, :])
        xinT = [b.sb(128, T, BF16) for _ in range(8)]
        uTt = [b.sb(128, 128, BF16) for _ in range(8)]
        for m in range(16):
            b.load(xt, xb[m * 128:(m + 1) * 128, :])
            b.lnorm(scr[:, 0:D], xt, scr2[:, 0:D], scr2[:, 1024:2048], mu, var, D, eps=1e-6)
            b.tt(scr[:, 0:D], scr[:, 0:D], nsb, ALU.mult)
            b.tt(scr[:, 0:D], scr[:, 0:D], nbb, ALU.add)
            b.cast(xnb, scr[:, 0:D])
            for j in range(8):
                b.tr128(xinT[j][:, m * 128:(m + 1) * 128], xnb[:, j * 128:(j + 1) * 128], psT, ident)
        load_ip(2048, 2048)
        for i in range(16):
            for ncn in range(4):
                b.mm(psB[:, :], [(ipb[k][:, i * 128:(i + 1) * 128], xinT[k][:, ncn * 512:(ncn + 1) * 512]) for k in range(8)])
                b.cast(xnb[:, 0:512], psB[:, :])
                b.store(z_scr[bb, i * 128:(i + 1) * 128, ncn * 512:(ncn + 1) * 512], xnb[:, 0:512])
        load_ip(0, 2048)
        for m in range(16):
            for ncn in range(4):
                b.mm(psB[:, :], [(xinT[k][:, m * 128:(m + 1) * 128], ipb[k][:, ncn * 512:(ncn + 1) * 512]) for k in range(8)])
                b.act(xhb[m][:, ncn * 512:(ncn + 1) * 512], psB[:, :], AF.Silu)
        load_ip(4096, 160)
        for m in range(16):
            b.mm(psB[:, 0:160], [(xinT[k][:, m * 128:(m + 1) * 128], ipb[k][:, 0:160]) for k in range(8)])
            b.copy(scr[:, 0:160], psB[:, 0:160])
            b.load(xt, xb[m * 128:(m + 1) * 128, :])
            b.cast(xnb, xt)
            for j in range(8):
                b.tr128(uTt[j], xnb[:, j * 128:(j + 1) * 128], psT, ident)
            b.mm(psB[:, 160:192], [(uTt[k][:, :], thb[k][:, :]) for k in range(8)])
            b.copy(scr[:, 160:192], psB[:, 160:192])
            b.tt(scr[:, 128:144], scr[:, 128:144], dtbb, ALU.add)
            b.act(scr2[:, 0:16], scr[:, 128:144], AF.Abs)
            b.act(scr2[:, 0:16], scr2[:, 0:16], AF.Exp, scale=-1.0)
            b.ch.op('vector', lambda E, d=scr2: E.tensor_scalar(d[:, 16:32], d[:, 0:16], 1.0, None, op0=ALU.add))
            b.act(scr2[:, 16:32], scr2[:, 16:32], AF.Ln)
            b.act(scr2[:, 32:48], scr[:, 128:144], AF.Relu)
            b.tt(scr[:, 128:144], scr2[:, 16:32], scr2[:, 32:48], ALU.add)
            b.act(scr[:, 144:160], scr[:, 144:160], AF.Sigmoid)
            b.tt(scr2[:, 0:16], scr[:, 128:144], Abc, ALU.mult)
            b.act(scr2[:, 0:16], scr2[:, 0:16], AF.Exp)
            b.tt(alc[m][:, :], scr[:, 128:144], scr[:, 144:160], ALU.mult)
            b.ts(scr2[:, 16:32], scr[:, 144:160], -1.0, ALU.mult, 1.0, ALU.add)
            b.tt(scr2[:, 16:32], scr2[:, 16:32], scr[:, 128:144], ALU.mult)
            b.tt(scr2[:, 16:32], scr2[:, 16:32], scr2[:, 0:16], ALU.mult)
            b.red(mu, scr[:, 128:144], 'add')
            b.ts(mu, mu, 1.0 / 16, ALU.mult)
            b.ts(scr[:, 160:192], scr[:, 160:192], mu[:, 0:1], ALU.mult)
            b.ch.op('vector', lambda E: E.tensor_copy(scr2[:, 48:64], scr[:, 128:144]))
            b.store(abk_scr[bb, m * 128:(m + 1) * 128, 0:16], scr2[:, 0:16])
            b.store(abk_scr[bb, m * 128:(m + 1) * 128, 16:32], scr2[:, 16:32])
            b.store(cd_scr[bb, m * 128:(m + 1) * 128, 16:48], scr[:, 160:192])
            b.store(cd_scr[bb, m * 128:(m + 1) * 128, 0:16], scr2[:, 48:64])
            b.copy(scr2[:, 128:224], scr[:, 0:96]) if False else None
            b.cast(Bb[m][:, :], scr[:, 0:64])
            b.cast(Cb[m][:, :], scr[:, 64:128])
        b.memset(scr[:, 0:48], 0.0)
        b.store(cd_scr[bb, T:T + 128, 0:48], scr[:, 0:48])
        b.memset(scr[:, 0:16], 1.0)
        b.store(abk_scr[bb, T:T + 128, 0:48], scr[:, 0:48])
        carry48 = b.sb(1, 48)
        b.memset(carry48, 0.0)
        stk = b.sb(128, 48)
        cdt = b.sb(128, 48)
        for m in range(16):
            b.load(stk[:, :], cd_scr[bb, m * 128:(m + 1) * 128, 0:48])
            b.mm(psCs[:, 0:48], [(ttri, stk[:, :]), (onesf[0:1, 0:128], carry48[0:1, :])])
            b.copy(cdt[:, :], psCs[:, 0:48])
            b.store(cd_scr[bb, m * 128:(m + 1) * 128, 0:48], cdt[:, :])
            b.ch.op('tensor', lambda E, sl=cdt[:, 0:48]: E.transpose(psTf[0:48, 0:128], sl, identf[:, :]))
            b.copy(scr[0:48, 0:128], psTf[0:48, 0:128])
            b.store(cdT_scr[bb, :, m * 128:(m + 1) * 128], scr[0:16, 0:128])
            b.ch.dma('sync', carry48[0:1, :], cdt[127:128, :])
            for half, (rawb, scb, bib) in enumerate(((Bb, bscb, bbib), (Cb, cscb, cbib))):
                src = rawb[m]
                b.cast(scr[:, 0:64], src[:, :])
                b.act(scr2[:, 0:64], scr[:, 0:64], AF.Square, accum=var[:, 0:1])
                b.ts(var[:, 0:1], var[:, 0:1], 1.0 / 64, ALU.mult, 1e-6, ALU.add)
                b.act(var[:, 0:1], var[:, 0:1], AF.Ln)
                b.act(var[:, 0:1], var[:, 0:1], AF.Exp, scale=-0.5)
                b.ts(scr[:, 0:64], scr[:, 0:64], var[:, 0:1], ALU.mult)
                b.tt(scr[:, 0:64], scr[:, 0:64], scb, ALU.mult)
                b.tt(scr[:, 0:64], scr[:, 0:64], bib, ALU.add)
                b.ts(scr2[:, 0:32], cdt[:, 16:48], 1.0 / (2 * np.pi), ALU.mult)
                b.cast(it32[:, 0:32], scr2[:, 0:32])
                b.cast(scr2[:, 32:64], it32[:, 0:32])
                b.tt(scr2[:, 0:32], scr2[:, 0:32], scr2[:, 32:64], ALU.subtract)
                b.ts(scr2[:, 32:64], scr2[:, 0:32], 0.0, ALU.is_lt)
                b.tt(scr2[:, 0:32], scr2[:, 0:32], scr2[:, 32:64], ALU.add)
                b.memset(var[:, 1:2], -float(np.pi))
                b.act(scr2[:, 32:64], scr2[:, 0:32], AF.Sin, scale=float(2 * np.pi), bias=var[:, 1:2])
                b.ts(scr2[:, 32:64], scr2[:, 32:64], -1.0, ALU.mult)
                b.ts(scr2[:, 64:96], scr2[:, 0:32], 0.25, ALU.add)
                b.cast(it32[:, 0:32], scr2[:, 64:96])
                b.cast(scr2[:, 96:128], it32[:, 0:32])
                b.tt(scr2[:, 64:96], scr2[:, 64:96], scr2[:, 96:128], ALU.subtract)
                b.ts(scr2[:, 96:128], scr2[:, 64:96], 0.0, ALU.is_lt)
                b.tt(scr2[:, 64:96], scr2[:, 64:96], scr2[:, 96:128], ALU.add)
                b.act(scr2[:, 96:128], scr2[:, 64:96], AF.Sin, scale=float(2 * np.pi), bias=var[:, 1:2])
                b.ts(scr2[:, 96:128], scr2[:, 96:128], -1.0, ALU.mult)
                b.tt(scr2[:, 128:160], scr[:, 0:32], scr2[:, 96:128], ALU.mult)
                b.tt(scr2[:, 160:192], scr[:, 32:64], scr2[:, 32:64], ALU.mult)
                b.tt(scr2[:, 128:160], scr2[:, 128:160], scr2[:, 160:192], ALU.subtract)
                b.tt(scr2[:, 160:192], scr[:, 0:32], scr2[:, 32:64], ALU.mult)
                b.tt(scr2[:, 192:224], scr[:, 32:64], scr2[:, 96:128], ALU.mult)
                b.tt(scr2[:, 160:192], scr2[:, 160:192], scr2[:, 192:224], ALU.add)
                b.cast(src[:, 0:32], scr2[:, 128:160])
                b.cast(src[:, 32:64], scr2[:, 160:192])
            b.ch.op('tensor', lambda E, sl=Bb[m][:, :]: E.transpose(psT[0:64, 0:128], sl, ident[:, :]))
            b.cast(BT[m][:, :], psT[0:64, 0:128])
            b.ch.op('tensor', lambda E, sl=Cb[m][:, :]: E.transpose(psT[0:64, 0:128], sl, ident[:, :]))
            b.cast(CT[m][:, :], psT[0:64, 0:128])
        for m in range(16):
            b.load(scr[:, 0:16], abk_scr[bb, m * 128 + 1:m * 128 + 129, 0:16])
            b.load(scr[:, 16:32], abk_scr[bb, m * 128 + 1:m * 128 + 129, 16:32])
            b.tt(kshc[m][:, :], scr[:, 0:16], alc[m][:, :], ALU.mult)
            b.tt(kshc[m][:, :], kshc[m][:, :], scr[:, 16:32], ALU.add)
            b.ch.op('vector', lambda E, d=scr: E.reciprocal(d[:, 32:48], d[:, 0:16]))
            b.tt(bashc[m][:, :], scr[:, 16:32], scr[:, 32:48], ALU.mult)
        b.pop()
        for h in range(NH):
            b.memset(hstf, 0.0)
            b.memset(prow[0:1, 0:1], 0.0)
            for m in range(16):
                b.load(crow[0:1, :], cdT_scr[bb, h, m * 128:(m + 1) * 128][None, :])
                b.load(cdsh[:, 0:16], cd_scr[bb, m * 128 + 1:m * 128 + 129, 0:16])
                b.bcast_row(Gb, crow, psB, onesf, 128)
                b.ts(Gb, Gb, cdsh[:, h:h + 1], ALU.subtract)
                b.ts(Gb, Gb, Abc[:, h:h + 1], ALU.mult, 0.0, ALU.min)
                b.act(Gb, Gb, AF.Exp)
                b.ts(Gb, Gb, kshc[m][:, h:h + 1], ALU.mult)
                b.tt(Gb, Gb, mstri, ALU.mult)
                b.ts(scr[:, 0:128], identf, alc[m][:, h:h + 1], ALU.mult)
                b.tt(Gb, Gb, scr[:, 0:128], ALU.add)
                b.mm(psM[:, :], [(BT[m][:, :], CT[m][:, :])])
                b.tt(scr[:, 0:128], psM[:, :], Gb, ALU.mult)
                b.ts(scr2[:, 0:128], identf, Dbc[:, h:h + 1], ALU.mult)
                b.tt(scr[:, 0:128], scr[:, 0:128], scr2[:, 0:128], ALU.add)
                b.cast(stb, scr[:, 0:128])
                b.mm(psY[:, :], [(xhb[m][:, h * 128:(h + 1) * 128], stb)])
                b.ts(erow[0:1, :], crow[0:1, :], prow[0:1, 0:1], ALU.subtract)
                b.ts(erow[0:1, :], erow[0:1, :], arow[0:1, h:h + 1], ALU.mult)
                b.act(erow[0:1, :], erow[0:1, :], AF.Exp)
                b.mm(psTf[0:64, 0:128], [(onesf[0:1, 0:64], erow[0:1, :])])
                b.cast(ebc, psTf[0:64, 0:128])
                b.tt(ebc, ebc, CT[m][:, :], ALU.mult)
                b.mm(psY2[:, :], [(hstf[:, :], ebc)])
                b.load(ztile, z_scr[bb, h * 128:(h + 1) * 128, m * 128:(m + 1) * 128])
                b.act(scr[:, 128:256], ztile, AF.Silu)
                b.copy(scr[:, 256:384], psY2[:, :])
                b.tt(scr[:, 0:128], psY[:, :], scr[:, 256:384], ALU.add)
                b.tt(scr[:, 0:128], scr[:, 0:128], scr[:, 128:256], ALU.mult)
                b.cast(ztile, scr[:, 0:128])
                b.store(yT_scr[bb, h * 128:(h + 1) * 128, m * 128:(m + 1) * 128], ztile)
                b.ts(scr[:, 128:129], oh127, bashc[m][:, h:h + 1], ALU.mult)
                b.tt(scr[:, 128:129], scr[:, 128:129], Gb[:, 127:128], ALU.add)
                b.ts(scr2[:, 0:64], Bb[m][:, :], scr[:, 128:129], ALU.mult)
                b.cast(xnb[:, 0:64], scr2[:, 0:64])
                b.mm(psSt[:, :], [(xnb[:, 0:64], xhb[m][:, h * 128:(h + 1) * 128])])
                b.ts(sc1[0:1, 0:1], crow[0:1, 127:128], prow[0:1, 0:1], ALU.subtract)
                b.ch.op('vector', lambda E: E.tensor_scalar(sc1[0:1, 0:1], sc1[0:1, 0:1], arow[0:1, h:h + 1], None, op0=ALU.mult))
                b.act(sc1[0:1, 0:1], sc1[0:1, 0:1], AF.Exp)
                b.mm(psTf[0:64, 0:1], [(onesf[0:1, 0:64], sc1[0:1, 0:1])])
                b.copy(var[0:64, 1:2], psTf[0:64, 0:1])
                b.ts(hstf, hstf, var[0:64, 1:2], ALU.mult)
                b.tt(hstf, hstf, psSt[:, :], ALU.add)
                b.copy(prow[0:1, 0:1], crow[0:1, 127:128])
        b.push()
        opb = [b.sb(128, D, BF16) for _ in range(16)]
        for i in range(16):
            b.ch.dma('gpsimd', opb[i][:, :], oph[i * 128:(i + 1) * 128, :])
        for m in range(16):
            for k in range(16):
                b.load(yTt[k], yT_scr[bb, k * 128:(k + 1) * 128, m * 128:(m + 1) * 128])
            b.load(xt, xb[m * 128:(m + 1) * 128, :])
            for ncn in range(2):
                sl = slice(ncn * 512, (ncn + 1) * 512)
                b.mm(psB[:, :], [(yTt[k][:, :], opb[k][:, sl]) for k in range(16)])
                b.tt(scr[:, sl], psB[:, :], xt[:, sl], ALU.add)
            b.store(xob[m * 128:(m + 1) * 128, :], scr[:, 0:D])
        b.pop()
    return b.finish(), b.consts


PROGS = {}


def _get(key, fn, *a):
    if key not in PROGS:
        PROGS[key] = fn(*a)
    return PROGS[key]


def _launch(nc, consts, per_core, outkey="xout"):
    in_maps = []
    for c in range(NCORE):
        m = dict(per_core[c])
        for k, v in consts.items():
            m[k] = v
        in_maps.append(m)
    res = run_bass_kernel_spmd(nc, in_maps, core_ids=list(range(NCORE)))
    return [r[outkey] for r in res.results]


def _f(a):
    return np.ascontiguousarray(a, dtype=np.float32)


def _shards(x):
    return [np.ascontiguousarray(x[c * BPC:(c + 1) * BPC]) for c in range(NCORE)]


def _go(key, buildfn, barg, x, wmap, outkey="xout"):
    nc, consts = _get(key, buildfn, *barg) if barg else _get(key, buildfn)
    xs = _shards(x)
    per_core = [dict({"xin": xs[c]}, **wmap) for c in range(NCORE)]
    outs = _launch(nc, consts, per_core, outkey)
    return np.concatenate(outs, axis=0)


def kernel(**ins):
    g = lambda n: _f(ins[n])
    x = _f(ins["L0_attn_a_x"])

    x = _go(("mamba1",), build_mamba1, None, x,
            {"win": g("L0_attn_a_W_in"), "cw": g("L0_attn_a_conv_w"), "cb": g("L0_attn_a_conv_b"),
             "wx": g("L0_attn_a_W_x"), "wdt": g("L0_attn_a_W_dt"), "bdt": g("L0_attn_a_b_dt"),
             "alog": g("L0_attn_a_A_log"), "dp": g("L0_attn_a_D_param"), "wout": g("L0_attn_a_W_out"),
             "lng": g("L0_attn_a_ln_g"), "lnb": g("L0_attn_a_ln_b")})

    def moe(mode, x, W1, b1, W2, b2, Wg, br=None, ga=None, be=None):
        nc, consts = _get(("moe", mode), build_moe, mode)
        xs = _shards(x)
        cur = x
        for half in range(2):
            sl = slice(half * 4, half * 4 + 4)
            curs = _shards(cur)
            per_core = []
            for c in range(NCORE):
                m = {"xin": xs[c], "acc": curs[c],
                     "W1": np.ascontiguousarray(W1[sl]), "b1": np.ascontiguousarray(b1[sl]),
                     "W2": np.ascontiguousarray(W2[sl]), "b2": np.ascontiguousarray(b2[sl]),
                     "wg": np.ascontiguousarray(np.roll(Wg, -4 * half, axis=1))}
                if mode == 'l2':
                    m["br"] = np.ascontiguousarray(np.roll(br, -4 * half))
                    m["ga"] = ga
                    m["be"] = be
                per_core.append(m)
            outs = _launch(nc, consts, per_core)
            cur = np.concatenate(outs, axis=0)
        return cur

    moe0w = (g("L0_ffn_W1"), g("L0_ffn_b1"), g("L0_ffn_W2"), g("L0_ffn_b2"), g("L0_ffn_Wg"))
    x = moe('l0', x, *moe0w)
    x = _go(("rotor",), build_rotor, None, x,
            {"wv": g("L0_attn_b_Wv"), "wo": g("L0_attn_b_Wo"), "fr": g("L0_attn_b_freqs")})
    x = moe('l0', x, *moe0w)

    aT = _go(("gqa",), build_gqa, None, x,
             {"wq": g("L1_attn_a_Wq"), "wk": g("L1_attn_a_Wk"), "wv": g("L1_attn_a_Wv")}, outkey="xoutT")
    nc, consts = _get(("ffnq", "swi"), build_ffnq, "swi")
    aTs = [np.ascontiguousarray(aT[c * BPC:(c + 1) * BPC]) for c in range(NCORE)]
    xs = _shards(x)
    per_core = [{"xinT": aTs[c], "xres": xs[c], "W1": g("L1_attn_a_W1"), "W2": g("L1_attn_a_W2")} for c in range(NCORE)]
    x = np.concatenate(_launch(nc, consts, per_core), axis=0)

    ffn1w = {"W1": g("L1_ffn_W1"), "b1": g("L1_ffn_b1"), "W2": g("L1_ffn_W2"), "b2": g("L1_ffn_b2")}
    x = _go(("ffnq", "l1"), build_ffnq, ("l1",), x, ffn1w)
    x = _go(("colattn",), build_colattn, None, x,
            {"mq": g("L1_attn_b_Mq"), "mk": g("L1_attn_b_Mk"), "mv": g("L1_attn_b_Mv")})
    x = _go(("ffnq", "l1"), build_ffnq, ("l1",), x, ffn1w)

    x = _go(("causal", "l2"), build_causal, ("l2",), x,
            {"qw": g("L2_attn_a_q_w"), "kw": g("L2_attn_a_k_w"), "vw": g("L2_attn_a_v_w"), "ow": g("L2_attn_a_o_w")})
    moe2w = (g("L2_ffn_W1"), g("L2_ffn_b1"), g("L2_ffn_W2"), g("L2_ffn_b2"), g("L2_ffn_Wr"),
             g("L2_ffn_br"), g("L2_ffn_gamma"), g("L2_ffn_beta"))
    x = moe('l2', x, *moe2w)
    x = _go(("linattn",), build_linattn, None, x,
            {"wqkv": g("L2_attn_b_W_qkv"), "wo": g("L2_attn_b_W_out")})
    x = moe('l2', x, *moe2w)

    x = _go(("causal", "l3"), build_causal, ("l3",), x,
            {"wa": g("L3_attn_a_w_attn"), "ba": g("L3_attn_a_b_attn"), "wp": g("L3_attn_a_w_proj"), "bp": g("L3_attn_a_b_proj")})
    ffn3w = {"W1": g("L3_ffn_W1"), "b1": g("L3_ffn_b1"), "W2": g("L3_ffn_W2"), "b2": g("L3_ffn_b2")}
    x = _go(("ffnq", "l3"), build_ffnq, ("l3",), x, ffn3w)
    x = _go(("ssd",), build_ssd, None, x,
            {"ns": g("L3_attn_b_norm_scale"), "nb": g("L3_attn_b_norm_bias"), "ip": g("L3_attn_b_in_proj_w"),
             "alog": g("L3_attn_b_A_log"), "dtb": g("L3_attn_b_dt_bias"), "bsc": g("L3_attn_b_bcB_scale"),
             "csc": g("L3_attn_b_bcC_scale"), "bbi": g("L3_attn_b_B_bias"), "cbi": g("L3_attn_b_C_bias"),
             "thw": g("L3_attn_b_theta_w"), "dd": g("L3_attn_b_D"), "op": g("L3_attn_b_out_proj_w")})
    x = _go(("ffnq", "l3"), build_ffnq, ("l3",), x, ffn3w)
    return x
